# version 30
# baseline (speedup 1.0000x reference)
import contextlib
import numpy as np

import concourse.bass as bass
from concourse.bass import IndirectOffsetOnAxis
import concourse.mybir as mybir
from concourse import bacc
from concourse.bass_utils import run_bass_kernel_spmd

F32 = mybir.dt.float32
BF16 = mybir.dt.bfloat16
I32 = mybir.dt.int32
I16 = mybir.dt.int16
U32 = mybir.dt.uint32
AF = mybir.ActivationFunctionType
ALU = mybir.AluOpType
AX = mybir.AxisListType

S = 4096
D = 1024
DI = 2048
DI2 = 4096
DS = 16
NE = 8
FF = 4096
EPS = 1e-5
NT = S // 128
TCH = 1024
NTC = S // TCH
NDB = DI // 128
NPAD = 768
NG = NPAD // 128
HPE = 4 * NG
WRC = NE * NPAD // 16


class Prog:

    def __init__(self):
        self.mode = "count"
        self.tags = {}
        self.counts = {}
        self.eng = None
        self.sems = {}
        self.vser = 0

    def start_count(self):
        self.mode = "count"
        self.counts = {}
        self.tags = {}
        self.vser = 0

    def start_emit(self, sems):
        self.mode = "emit"
        self.sems = sems
        self.vser = 0

    def vtag(self):
        self.vser += 1
        return ("_f", self.vser)

    def op(self, fn, inc=None):
        incs = [] if inc is None else (inc if isinstance(inc, list) else [inc])
        if self.mode == "count":
            for (sem, tag, n) in incs:
                c = self.counts.get(sem, 0) + n
                self.counts[sem] = c
                assert (sem, tag) not in self.tags, f"dup tag {sem} {tag}"
                self.tags[(sem, tag)] = c
        else:
            r = fn(self.eng)
            for (sem, tag, n) in incs:
                r.then_inc(self.sems[sem], n)

    def wait(self, sem, tag):
        if self.mode == "emit":
            v = self.tags.get((sem, tag))
            assert v is not None, f"missing tag {sem} {tag}"
            self.eng.wait_ge(self.sems[sem], v)

    def wait_maybe(self, sem, tag):
        if self.mode == "emit":
            v = self.tags.get((sem, tag))
            if v is not None:
                self.eng.wait_ge(self.sems[sem], v)

    def barrier(self, k):
        if self.mode == "emit":
            self.eng.nop().then_inc(self.sems["phb"], 1)
            self.eng.wait_ge(self.sems["phb"], 5 * k)


SEMS = ["phb", "vself", "aself", "gi",
        "sstg", "sprm", "g3",
        "sx0", "sx1", "sac0", "sac1", "gx0", "gx1", "v0", "a0", "t0",
        "gw0", "gw1", "gw2", "gw3", "sw0", "sw1", "gxs0", "gxs1", "gsz0", "gsz1", "t1", "v1", "a1",
        "sxst0", "sxst1", "t3", "v3",
        "gy0", "gy1", "v4", "a4",
        "scq0", "scq1", "sbq0", "sbq1", "sdr0", "sdr1", "ssg0", "ssg1",
        "sout0", "sout1",
        "gpm", "gidx", "ggx0", "gsc",
        "sy20", "sy21", "sy22", "sal0", "sal1", "t5", "v5",
        "g60", "g61", "t6", "v6", "a6"]


DBG = {}


def build_nc(phases=7):
    nc = bass.Bass("TRN2", target_bir_lowering=False)

    x_h = nc.dram_tensor("x", [S, D], F32, kind="ExternalInput").ap()
    lng_h = nc.dram_tensor("ln_g", [8, 128], F32, kind="ExternalInput").ap()
    lnb_h = nc.dram_tensor("ln_b", [8, 128], F32, kind="ExternalInput").ap()
    inw_h = nc.dram_tensor("in_w", [DI2, D], F32, kind="ExternalInput").ap()
    convw_h = nc.dram_tensor("conv_w", [DI, 4], F32, kind="ExternalInput").ap()
    convb_h = nc.dram_tensor("conv_b", [16, 128], F32, kind="ExternalInput").ap()
    xpw_h = nc.dram_tensor("xproj_w", [33, DI], F32, kind="ExternalInput").ap()
    dtw_h = nc.dram_tensor("dt_w", [16, 128], F32, kind="ExternalInput").ap()
    dtb_h = nc.dram_tensor("dt_b", [16, 128], F32, kind="ExternalInput").ap()
    alog_h = nc.dram_tensor("A_log", [1, DS], F32, kind="ExternalInput").ap()
    dvec_h = nc.dram_tensor("Dvec", [16, 128], F32, kind="ExternalInput").ap()
    outw_h = nc.dram_tensor("out_w", [D, DI], F32, kind="ExternalInput").ap()
    wr_h = nc.dram_tensor("wr", [D, NE], F32, kind="ExternalInput").ap()
    br_h = nc.dram_tensor("br", [NE, 1], F32, kind="ExternalInput").ap()
    w1_h = nc.dram_tensor("w1", [NE, D, FF], F32, kind="ExternalInput").ap()
    b1_h = nc.dram_tensor("b1", [NE * 32, 128], F32, kind="ExternalInput").ap()
    w2_h = nc.dram_tensor("w2", [NE, FF, D], F32, kind="ExternalInput").ap()
    b2_h = nc.dram_tensor("b2", [1, NE * D], F32, kind="ExternalInput").ap()
    out_h = nc.dram_tensor("out", [S, D], F32, kind="ExternalOutput").ap()

    xn_cm_h = nc.dram_tensor("xn_cm", [128, 8 * S], BF16, kind="Internal").ap()
    xs_hbm = nc.dram_tensor("xs_sc", [128, NDB * S], BF16, kind="Internal").ap()
    sz_hbm = nc.dram_tensor("sz_sc", [128, NDB * S], BF16, kind="Internal").ap()
    y2_hbm = nc.dram_tensor("y2_sc", [128, NDB * S], BF16, kind="Internal").ap()
    acc_hbm = nc.dram_tensor("acc_sc", [S + 1, D], BF16, kind="Internal").ap()
    drrow_h = nc.dram_tensor("dr_sc", [1, S], BF16, kind="Internal").ap()
    rowbc_h = nc.dram_tensor("bc_sc", [32, S], BF16, kind="Internal").ap()
    bbc_h = nc.dram_tensor("bbc_sc", [64, TCH], BF16, kind="Internal").ap()
    cbc_h = nc.dram_tensor("cbc_sc", [64, TCH], BF16, kind="Internal").ap()
    arow_h = nc.dram_tensor("a_sc", [1, DS], F32, kind="Internal").ap()
    dA_hbm = nc.dram_tensor("dA_sc", [128, NDB * DS * S], BF16, kind="Internal").ap()
    xn_tm_h = nc.dram_tensor("xn_tm", [S + 1, D], BF16, kind="Internal").ap()
    perm_h = nc.dram_tensor("perm_sc", [NE * NPAD, 1], I32, kind="Internal").ap()
    u_hbm = nc.dram_tensor("u_sc", [128, NDB * S], BF16, kind="Internal").ap()

    es = contextlib.ExitStack()
    with es:
        sbt = lambda name, shape, dt: es.enter_context(nc.sbuf_tensor(name, shape, dt))
        pst = lambda name, shape, dt: es.enter_context(nc.psum_tensor(name, shape, dt))

        identf = sbt("identf", [128, 128], F32)
        identb = sbt("identb", [128, 128], BF16)

        g_col = sbt("g_col", [128, 8], F32)
        bcol = sbt("bcol", [128, 8], F32)
        convw_col = sbt("convw_col", [128, 64], F32)
        convb_col = sbt("convb_col", [128, 16], F32)
        dtw_col = sbt("dtw_col", [128, 16], F32)
        dtb_col = sbt("dtb_col", [128, 16], F32)
        d_col = sbt("d_col", [128, 16], F32)
        a_col = sbt("a_col", [128, DS], F32)
        wr_sb = sbt("wr_sb", [128, 8 * NE], F32)
        br_sb = sbt("br_sb", [NE, 1], F32)
        b1_col = sbt("b1_col", [128, NE * 32], F32)
        b2cur = [sbt("b2c0", [1, D], BF16), sbt("b2c1", [1, D], BF16)]
        ones1 = sbt("ones1", [1, 128], BF16)
        epsrow = sbt("epsrow", [128, NE], F32)
        oh_tok = sbt("oh_tok", [128, NT * NE], F32)
        ones8c = sbt("ones8c", [128, NE], F32)
        erow = sbt("erow", [128, NE], F32)
        tokid = sbt("tokid", [128, NT], I32)
        carry = sbt("carry", [128, NDB * DS], F32)


        AR1 = sbt("AR1", [128, 15872], F32)
        AR2 = sbt("AR2", [128, 18432], F32)
        AR3 = sbt("AR3", [128, 16384], F32)
        small = sbt("small", [128, 128], F32)

        def cv(arena, off, n, dt=F32):
            v = arena[:, off:off + n]
            return v if dt is F32 else v.bitcast(dt)

        idx_i = cv(AR1, 0, 128).bitcast(I32)
        pstage = cv(AR1, 128, 128)

        red1 = small[:, 0:1]; red2 = small[:, 1:2]
        mean1 = small[:, 2:3]; var1 = small[:, 4:5]
        t_a = small[:, 6:7]; t_b = small[:, 7:8]
        mean2 = small[:, 8:9]; var2 = small[:, 10:11]
        rstd1 = small[:, 12:13]; rstd2 = small[:, 13:14]
        lntmp = small[:, 14:15]; lntmp2 = small[:, 15:16]
        mx8 = small[:, 16:17]
        ps8 = small[:, 80:88]
        e18 = small[:, 88:96]
        zcol = small[:, 96:97]
        bn6 = small[:, 98:110]
        bnag = small[:, 110:112]
        tmp8a = small[:, 98:106]
        tmp8b = small[:, 106:114]
        r1c = small[:, 114:115]
        r2c = small[:, 115:116]
        slc = small[:, 116:117]
        ovc = small[:, 117:118]
        sic = small[:, 118:119].bitcast(I32)
        qic = small[:, 119:120].bitcast(I32)
        cic = small[:, 120:121].bitcast(I32)
        adj8 = small[:, 24:32]
        sc8 = small[:, 32:40]
        astage = small[0:1, 40:56]
        aneg = small[0:1, 56:72]
        vsyncb = small[:, 72:73]
        cEPS = small[:, 73:74]
        cM05 = small[:, 74:75]
        cM1 = small[:, 75:76]
        cONE = small[:, 76:77]

        PSA = pst("PSA", [128, 1024], F32)
        PSB = pst("PSB", [128, 1024], F32)
        PSC = pst("PSC", [128, 1024], F32)
        PSD = pst("PSD", [128, 1024], F32)

        DBG.clear()
        DBG.update(dict(acc_hbm=acc_hbm, perm_h=perm_h, xn_tm_h=xn_tm_h, xn_cm_h=xn_cm_h, xs_hbm=xs_hbm, sz_hbm=sz_hbm, out_h=out_h, dA_hbm=dA_hbm, u_hbm=u_hbm,
                        y2_hbm=y2_hbm, drrow_h=drrow_h, rowbc_h=rowbc_h))
        p = Prog()

        PLIST = [("lng", 0, 8, g_col), ("lnb", 0, 8, bcol), ("convb", 0, 16, convb_col),
                 ("dtw", 0, 16, dtw_col), ("dtb", 0, 16, dtb_col), ("dvec", 0, 16, d_col)]

        def p0(which):
            if which == "sync":
                for kt in range(8):
                    p.op(lambda e, kt=kt: e.dma_start(out=wr_sb[:, kt * NE:(kt + 1) * NE],
                                                      in_=wr_h[kt * 128:(kt + 1) * 128, :]),
                         inc=("sprm", ("wr", kt), 16))
                p.op(lambda e: e.dma_start(out=br_sb[:, :], in_=br_h[:, :]), inc=("sprm", "br", 16))
                p.op(lambda e: e.dma_start(out=convw_col.rearrange("q (db j) -> q db j", j=4),
                                           in_=convw_h.rearrange("(db q) j -> q db j", q=128)),
                     inc=("sprm", "convw", 16))
                p.op(lambda e: e.dma_start(out=astage, in_=alog_h[:, :]), inc=("sprm", "alog", 16))
                for idx2, ((nm, r0, k), srcap) in enumerate(zip([(n, r, k) for n, r, k, _ in PLIST],
                                            [lng_h, lnb_h, convb_h, dtw_h, dtb_h, dvec_h])):
                    if idx2 > 0:
                        p.wait("t3", ("pT", PLIST[idx2 - 1][0]))
                    p.op(lambda e, k=k, srcap=srcap: e.dma_start(out=pstage[0:k, :], in_=srcap[:, :]),
                         inc=("sstg", ("p0", nm), 16))
                p.wait("t3", ("pT", PLIST[-1][0]))
                for ee in range(NE):
                    if ee > 0:
                        p.wait("t3", ("b1T", ee - 1))
                    p.op(lambda e, ee=ee: e.dma_start(out=pstage[32:64, :],
                                                      in_=b1_h[ee * 32:(ee + 1) * 32, :]),
                         inc=("sstg", ("b1s", ee), 16))
                p.wait("sprm", "alog")
                p.wait("sstg", ("b1s", NE - 1))
            elif which == "gpsimd":
                p.op(lambda e: e.iota(idx_i[:, :], [[1, 128]], base=0, channel_multiplier=-1),
                     inc=("gi", "iota", 1))
                p.op(lambda e: e.iota(erow[:, :], [[1, NE]], base=0, channel_multiplier=0,
                                      allow_small_or_imprecise_dtypes=True),
                     inc=("gi", "iotae", 1))
                p.op(lambda e: e.iota(tokid[:, :], [[128, NT]], base=0, channel_multiplier=1),
                     inc=("gi", "iotat", 1))

                p.wait("vself", ("p0", "aneg"))
                p.op(lambda e: e.dma_start(out=arow_h[:, :], in_=aneg), inc=("g3", "arowst", 16))
                p.wait("g3", "arowst")
                p.op(lambda e: e.dma_start(out=a_col[:, :],
                                           in_=arow_h[:, :].partition_broadcast(128).squeeze(1)),
                     inc=("g3", "acol", 16))
                p.wait("g3", "acol")
            elif which == "scalar":
                p.wait("sprm", "alog")
                p.op(lambda e: e.activation(aneg, astage, AF.Exp), inc=("aself", ("p0", "expa"), 1))
            elif which == "vector":
                p.wait("gi", "iota")
                p.op(lambda e: e.tensor_scalar(identf[:, :], idx_i[:, :], 0, None, op0=ALU.is_equal),
                     inc=("vself", ("p0", "idf"), 1))
                p.op(lambda e: e.tensor_scalar(identb[:, :], idx_i[:, :], 0, None, op0=ALU.is_equal),
                     inc=("vself", ("p0", "idb"), 1))
                p.op(lambda e: e.tensor_scalar(epsrow[:, :], idx_i[:, 0:NE], 1e-30, None, op0=ALU.mult),
                     inc=("vself", ("p0", "epsr"), 1))
                p.op(lambda e: e.memset(carry[:, :], 0.0))
                p.op(lambda e: e.memset(ones1[:, :], 1.0))
                p.op(lambda e: e.memset(ones8c[:, :], 1.0))
                p.op(lambda e: e.memset(zcol, 0.0))
                p.wait("gi", "iotae")
                p.op(lambda e: e.tensor_scalar(erow[:, :], erow[:, :], float(NPAD), None, op0=ALU.mult),
                     inc=("vself", ("p0", "erow"), 1))
                p.op(lambda e: e.memset(cEPS, EPS))
                p.op(lambda e: e.memset(cM05, -0.5))
                p.op(lambda e: e.memset(cM1, -1.0))
                p.op(lambda e: e.memset(cONE, 1.0))
                for nm, _, k, dst in PLIST:
                    p.wait("t3", ("pT", nm))
                    p.op(lambda e, dst=dst, k=k: e.tensor_copy(dst[:, 0:k], PSC[:, 0:k]),
                         inc=("vself", ("pev", nm), 1))
                for ee in range(NE):
                    p.wait("t3", ("b1T", ee))
                    p.op(lambda e, ee=ee: e.tensor_copy(b1_col[:, ee * 32:(ee + 1) * 32], PSA[:, 0:32]),
                         inc=("vself", ("evb1", ee), 1))
                p.wait("aself", ("p0", "expa"))
                p.op(lambda e: e.tensor_scalar(aneg, aneg, -1.0, None, op0=ALU.mult),
                     inc=("vself", ("p0", "aneg"), 1))
            elif which == "tensor":
                p.wait("vself", ("p0", "idf"))
                for i, (nm, r0, k, _) in enumerate(PLIST):
                    p.wait("sstg", ("p0", nm))
                    if i > 0:
                        p.wait("vself", ("pev", PLIST[i - 1][0]))
                    p.op(lambda e, k=k: e.transpose(PSC[:, 0:k], pstage[0:k, :], identf[0:k, 0:k]),
                         inc=("t3", ("pT", nm), 1))
                for ee in range(NE):
                    p.wait("sstg", ("b1s", ee))
                    if ee > 0:
                        p.wait("vself", ("evb1", ee - 1))
                    p.op(lambda e: e.transpose(PSA[:, 0:32], pstage[32:64, :], identf[32:64, 32:64]),
                         inc=("t3", ("b1T", ee), 1))

        xn2_sb = cv(AR2, 0, 16384, BF16)
        xbuf = [cv(AR1, 0, 1024), cv(AR1, 1024, 1024)]
        sqb = cv(AR1, 2048, 1024)
        xnbuf = [cv(AR1, 3072, 1024), cv(AR1, 4096, 1024)]
        xnn2b = [cv(AR1, 5120, 1024), cv(AR1, 6144, 1024)]
        accb = [cv(AR1, 7168, 512, BF16), cv(AR1, 7680, 512, BF16)]
        xnT = [cv(AR1, 9216, 1024), cv(AR1, 10240, 1024)]
        sc_eb = cv(AR1, 11264, 128)
        oh_et = cv(AR3, 0, 4096)
        pos_et = cv(AR3, 4096, 4096)
        slotub = cv(AR3, 8224, 32).bitcast(U32)
        slotib = cv(AR3, 8224, 32).bitcast(I32)
        zer8 = cv(AR3, 8256, 1)
        ones_et = cv(AR3, 8448, 4096)
        sent_f = cv(AR3, 12544, 128)
        sent_i = cv(AR3, 12672, 128).bitcast(I32)
        zrow16 = cv(AR3, 12800, 512, BF16)
        ps_t = [PSA, PSB]
        ps_t2 = PSC
        ps_sc = PSD[:, 0:128]
        ps_scT = PSD[:, 512:520]
        ps_ohT = PSD[:, 640:768]
        ps_psT = PSD[:, 528:536]

        def p1v_A(i):
            par = i % 2
            p.wait("sx0" if par == 0 else "sx1", ("x", i))
            if i >= 2:
                p.wait("sac0" if par == 0 else "sac1", ("st", i - 2))
            t1, t2, t3_, t4, t5, t6 = [p.vtag() for _ in range(6)]
            p.op(lambda e, par=par: e.tensor_tensor(sqb[:, :], xbuf[par][:, :], xbuf[par][:, :],
                                                    op=ALU.mult), inc=("vself", t1, 1))
            p.op(lambda e, par=par: e.reduce_sum(red1, xbuf[par][:, :], axis=AX.X),
                 inc=("vself", t2, 1))
            p.wait("vself", t1)
            p.op(lambda e: e.reduce_sum(red2, sqb[:, :], axis=AX.X), inc=("vself", t3_, 1))
            p.wait("vself", t2)
            p.op(lambda e: e.tensor_scalar(mean1, red1, 1.0 / D, None, op0=ALU.mult),
                 inc=("vself", t4, 1))
            p.wait("vself", t3_)
            p.op(lambda e: e.tensor_scalar(t_a, red2, 1.0 / D, None, op0=ALU.mult),
                 inc=("vself", t5, 1))
            p.wait("vself", t4)
            p.op(lambda e: e.tensor_tensor(t_b, mean1, mean1, op=ALU.mult), inc=("vself", t6, 1))
            p.wait("vself", t5)
            p.wait("vself", t6)
            p.op(lambda e: e.tensor_tensor(var1, t_a, t_b, op=ALU.subtract),
                 inc=("v0", ("var1", i), 1))

        def p1v_B(i):
            par = i % 2
            p.wait("a0", ("rstd1", i))
            if i >= 2:
                p.wait("gx0" if par == 0 else "gx1", ("sp", i - 2))
            p.op(lambda e, par=par: e.tensor_scalar(xnbuf[par][:, :], xbuf[par][:, :], mean1, rstd1,
                                                    op0=ALU.subtract, op1=ALU.mult),
                 inc=("v0", ("xn", i), 1))
            p.wait("v0", ("xn", i))
            p.op(lambda e, par=par: e.tensor_tensor(accb[par][:, :], xbuf[par][:, :], xnbuf[par][:, :],
                                                    op=ALU.add), inc=("v0", ("acc", i), 1))
            u1, u2, u3, u4, u5, u6 = [p.vtag() for _ in range(6)]
            p.op(lambda e, par=par: e.tensor_tensor(sqb[:, :], xnbuf[par][:, :], xnbuf[par][:, :],
                                                    op=ALU.mult), inc=("vself", u1, 1))
            p.op(lambda e, par=par: e.reduce_sum(red1, xnbuf[par][:, :], axis=AX.X),
                 inc=("vself", u2, 1))
            p.wait("vself", u1)
            p.op(lambda e: e.reduce_sum(red2, sqb[:, :], axis=AX.X), inc=("vself", u3, 1))
            p.wait("vself", u2)
            p.op(lambda e: e.tensor_scalar(mean2, red1, 1.0 / D, None, op0=ALU.mult),
                 inc=("vself", u4, 1))
            p.wait("vself", u3)
            p.op(lambda e: e.tensor_scalar(t_a, red2, 1.0 / D, None, op0=ALU.mult),
                 inc=("vself", u5, 1))
            p.wait("vself", u4)
            p.op(lambda e: e.tensor_tensor(t_b, mean2, mean2, op=ALU.mult), inc=("vself", u6, 1))
            p.wait("vself", u5)
            p.wait("vself", u6)
            p.op(lambda e: e.tensor_tensor(var2, t_a, t_b, op=ALU.subtract),
                 inc=("v0", ("var2", i), 1))

        def p1v_C(i):
            par = i % 2
            p.wait("a0", ("rstd2", i))
            p.op(lambda e, par=par: e.tensor_scalar(xnn2b[par][:, :], xnbuf[par][:, :], mean2, rstd2,
                                                    op0=ALU.subtract, op1=ALU.mult),
                 inc=("v0", ("xnn2", i), 1))

        def p1v_D(i):
            p.wait("t0", ("scT", i))
            w1t, w2t, w3t = p.vtag(), p.vtag(), p.vtag()
            p.op(lambda e: e.tensor_copy(sc8, ps_scT[:, :]), inc=("vself", w1t, 1))
            p.wait("vself", w1t)
            p.op(lambda e: e.tensor_tensor(adj8, sc8, epsrow[:, :], op=ALU.subtract),
                 inc=("vself", w2t, 1))
            p.wait("vself", w2t)
            p.op(lambda e: e.reduce_max(mx8, adj8, axis=AX.X), inc=("vself", w3t, 1))
            p.wait("vself", w3t)
            w4t, w5t, w6t = p.vtag(), p.vtag(), p.vtag()
            p.op(lambda e, i=i: e.tensor_scalar(oh_tok[:, i * NE:(i + 1) * NE], adj8, mx8, None,
                                                op0=ALU.is_equal), inc=("vself", w4t, 1))
            p.wait("vself", w4t)
            p.op(lambda e, i=i: e.tensor_tensor_scan(ps8, ones8c[:, :],
                                                     oh_tok[:, i * NE:(i + 1) * NE],
                                                     zcol, op0=ALU.mult, op1=ALU.add),
                 inc=("vself", w5t, 1))
            p.wait("vself", w5t)
            p.op(lambda e: e.tensor_scalar(e18, ps8, 1.0, None, op0=ALU.is_equal),
                 inc=("vself", w6t, 1))
            p.wait("vself", w6t)
            p.op(lambda e, i=i: e.tensor_tensor(oh_tok[:, i * NE:(i + 1) * NE],
                                                oh_tok[:, i * NE:(i + 1) * NE], e18,
                                                op=ALU.mult), inc=("v0", ("oh", i), 1))

        def p1a_rstd1(i):
            p.wait("v0", ("var1", i))
            at = ("af1", i)
            p.op(lambda e: e.activation(lntmp, var1, AF.Ln, bias=cEPS), inc=("aself", at, 1))
            p.wait("aself", at)
            p.op(lambda e: e.activation(rstd1, lntmp, AF.Exp, scale=cM05),
                 inc=("a0", ("rstd1", i), 1))

        def p1a_rstd2(i):
            p.wait("v0", ("var2", i))
            at2 = ("af2", i)
            p.op(lambda e: e.activation(lntmp2, var2, AF.Ln, bias=cEPS), inc=("aself", at2, 1))
            p.wait("aself", at2)
            p.op(lambda e: e.activation(rstd2, lntmp2, AF.Exp, scale=cM05),
                 inc=("a0", ("rstd2", i), 1))

        def p1a_evacs(i):
            par = i % 2
            p.wait("t0", ("xnT", i))
            p.op(lambda e, par=par: e.activation(xnT[par][:, :], ps_t[par][:, :], AF.Copy),
                 inc=("a0", ("evT", i), 1))
            p.wait("t0", ("xnn2T", i))
            for j in range(8):
                inc = ("a0", ("aff", i), 1) if j == 7 else None
                p.op(lambda e, i=i, j=j: e.activation(
                    xn2_sb[:, j * S + i * 128:j * S + (i + 1) * 128],
                    ps_t2[:, j * 128:(j + 1) * 128], AF.Identity,
                    scale=g_col[:, j:j + 1], bias=bcol[:, j:j + 1]), inc=inc)
            p.wait("t0", ("mmsc", i))
            p.op(lambda e: e.activation(sc_eb[0:NE, :], ps_sc[0:NE, :], AF.Identity,
                                        bias=br_sb[:, 0:1]), inc=("a0", ("scev", i), 1))
            p.wait("t0", ("ohT", i))
            p.op(lambda e, i=i: e.activation(oh_et[0:NE, i * 128:(i + 1) * 128],
                                             ps_ohT[0:NE, :], AF.Copy),
                 inc=("a0", ("ohet", i), 1))

        def p1(which):
            if which == "scalar":
                p1a_rstd1(0)
                for i in range(NT):
                    p1a_rstd2(i)
                    if i + 1 < NT:
                        p1a_rstd1(i + 1)
                    p1a_evacs(i)
            if which == "vector":
                p1v_A(0)
                for i in range(NT):
                    p1v_B(i)
                    if i >= 1:
                        p1v_D(i - 1)
                    if i + 1 < NT:
                        p1v_A(i + 1)
                    p1v_C(i)
                p1v_D(NT - 1)
            for i in range(NT):
                par = i % 2
                if which == "sync":
                    if i >= 2:
                        p.wait("v0", ("acc", i - 2))
                    p.op(lambda e, i=i, par=par: e.dma_start(out=xbuf[par][:, :],
                                                             in_=x_h[i * 128:(i + 1) * 128, :]),
                         inc=("sx0" if par == 0 else "sx1", ("x", i), 16))
                    p.wait("v0", ("acc", i))
                    p.op(lambda e, i=i, par=par: e.dma_start(out=acc_hbm[i * 128:(i + 1) * 128, :],
                                                             in_=accb[par][:, :]),
                         inc=("sac0" if par == 0 else "sac1", ("st", i), 16))
                    if i == NT - 1:
                        p.wait("sx0", ("x", NT - 2))
                        p.wait("sx1", ("x", NT - 1))
                        p.wait("sac0", ("st", NT - 2))
                        p.wait("sac1", ("st", NT - 1))
                elif which == "gpsimd":
                    p.wait("v0", ("xn", i))
                    p.op(lambda e, i=i, par=par: e.dma_start(
                        out=xn_tm_h[i * 128:(i + 1) * 128, :],
                        in_=xnbuf[par][:, :]),
                        inc=("gx0" if par == 0 else "gx1", ("sp", i), 16))
                    if i == NT - 1:
                        p.wait("gx0", ("sp", NT - 2))
                        p.wait("gx1", ("sp", NT - 1))
                elif which == "vector":
                    pass
                elif which == "scalar":
                    pass
                elif which == "tensor":
                    p.wait("v0", ("xn", i))
                    if i >= 2:
                        p.wait("a0", ("evT", i - 2))
                    for j in range(8):
                        inc = ("t0", ("xnT", i), 1) if j == 7 else None
                        p.op(lambda e, par=par, j=j: e.transpose(ps_t[par][:, j * 128:(j + 1) * 128],
                                                                 xnbuf[par][:, j * 128:(j + 1) * 128],
                                                                 identf[:, :]), inc=inc)
                    p.wait("v0", ("xnn2", i))
                    if i >= 1:
                        p.wait("a0", ("aff", i - 1))
                    for j in range(8):
                        inc = ("t0", ("xnn2T", i), 1) if j == 7 else None
                        p.op(lambda e, par=par, j=j: e.transpose(ps_t2[:, j * 128:(j + 1) * 128],
                                                                 xnn2b[par][:, j * 128:(j + 1) * 128],
                                                                 identf[:, :]), inc=inc)
                    p.wait("a0", ("evT", i))
                    if i >= 1:
                        p.wait("v0", ("oh", i - 1))
                    for j in range(8):
                        inc = ("t0", ("mmsc", i), 1) if j == 7 else None
                        p.op(lambda e, par=par, j=j: e.matmul(ps_sc[0:NE, :], wr_sb[:, j * NE:(j + 1) * NE],
                                                              xnT[par][:, j * 128:(j + 1) * 128],
                                                              start=(j == 0), stop=(j == 7)), inc=inc)
                    p.wait("a0", ("scev", i))
                    if i >= 1:
                        p.wait("v0", ("oh", i - 1))
                    p.op(lambda e: e.transpose(ps_scT[:, :], sc_eb[0:NE, :], identf[0:NE, 0:NE]),
                         inc=("t0", ("scT", i), 1))
                    p.wait("v0", ("oh", i))
                    if i >= 1:
                        p.wait("a0", ("ohet", i - 1))
                    p.op(lambda e, i=i: e.transpose(ps_ohT[0:NE, :], oh_tok[:, i * NE:(i + 1) * NE],
                                                    identf[:, :]), inc=("t0", ("ohT", i), 1))

            if which == "vector":
                vt1, vt2 = p.vtag(), p.vtag()
                p.op(lambda e: e.memset(ones_et[0:8, :], 1.0))
                p.op(lambda e: e.memset(zer8[0:8, 0:1], 0.0), inc=("vself", vt1, 1))
                p.op(lambda e: e.memset(sent_f[:, 0:NE * NPAD // 128], float(S)), inc=("vself", vt2, 1))
                p.wait("vself", vt2)
                p.op(lambda e: e.tensor_scalar(sent_i[:, 0:NE * NPAD // 128],
                                               sent_f[:, 0:NE * NPAD // 128], 0.0, None,
                                               op0=ALU.add), inc=("v0", ("sent",), 1))
                p.op(lambda e: e.memset(zrow16[0:1, :], 0.0), inc=("v0", ("zrow",), 1))
                p.wait("vself", vt1)
                p.wait("a0", ("ohet", NT - 1))
                p.op(lambda e: e.tensor_tensor_scan(pos_et[0:8, :], ones_et[0:8, :], oh_et[0:8, :],
                                                    zer8[0:8, 0:1], op0=ALU.mult, op1=ALU.add),
                     inc=("v0", ("pos",), 1))
                for i in range(NT):
                    w = [p.vtag() for _ in range(8)]
                    p.wait("t0", ("psT", i))
                    p.op(lambda e, i=i: e.tensor_tensor(tmp8a, ps_psT,
                                                        oh_tok[:, i * NE:(i + 1) * NE], op=ALU.mult),
                         inc=("v0", ("slr", i), 1))
                    p.op(lambda e, i=i: e.tensor_tensor(tmp8b, erow[:, :],
                                                        oh_tok[:, i * NE:(i + 1) * NE], op=ALU.mult),
                         inc=("vself", w[0], 1))
                    p.wait("v0", ("slr", i))
                    p.op(lambda e: e.reduce_sum(r1c, tmp8a, axis=AX.X), inc=("vself", w[1], 1))
                    p.wait("vself", w[0])
                    p.op(lambda e: e.reduce_sum(r2c, tmp8b, axis=AX.X), inc=("vself", w[2], 1))
                    p.wait("vself", w[1])
                    p.wait("vself", w[2])
                    p.op(lambda e: e.tensor_tensor(slc, r1c, r2c, op=ALU.add), inc=("vself", w[3], 1))
                    p.op(lambda e: e.tensor_scalar(ovc, r1c, float(NPAD), 0.0,
                                                   op0=ALU.subtract, op1=ALU.max),
                         inc=("vself", w[4], 1))
                    p.wait("vself", w[4])
                    p.op(lambda e: e.tensor_scalar(ovc, ovc, 1.0e6, None, op0=ALU.mult),
                         inc=("vself", w[5], 1))
                    p.wait("vself", w[3])
                    p.wait("vself", w[5])
                    p.op(lambda e: e.tensor_tensor(slc, slc, ovc, op=ALU.add), inc=("vself", w[6], 1))
                    p.wait("vself", w[6])
                    p.op(lambda e: e.tensor_scalar(slc, slc, 1.0, None, op0=ALU.subtract),
                         inc=("vself", w[7], 1))
                    w8, w9, w10, w11, w12 = [p.vtag() for _ in range(5)]
                    p.wait("vself", w[7])
                    p.op(lambda e: e.tensor_scalar(slc, slc, 131072.0, None, op0=ALU.min),
                         inc=("vself", w8, 1))
                    p.wait("vself", w8)
                    p.op(lambda e: e.tensor_scalar(sic, slc, 0.0, None, op0=ALU.add),
                         inc=("vself", w9, 1))
                    p.wait("vself", w9)
                    p.op(lambda e: e.tensor_scalar(qic, sic, 15, None, op0=ALU.bitwise_and),
                         inc=("vself", w10, 1))
                    p.op(lambda e: e.tensor_scalar(cic, sic, 4, None, op0=ALU.logical_shift_right),
                         inc=("vself", w11, 1))
                    p.wait("vself", w10)
                    p.op(lambda e: e.tensor_scalar(qic, qic, WRC, None, op0=ALU.mult),
                         inc=("vself", w12, 1))
                    p.wait("vself", w11)
                    p.wait("vself", w12)
                    p.op(lambda e, i=i: e.tensor_tensor(slotib[:, i:i + 1], qic, cic, op=ALU.add),
                         inc=("v0", ("slot", i), 1))
            elif which == "tensor":
                p.wait("v0", ("pos",))
                for i in range(NT):
                    if i >= 1:
                        p.wait("v0", ("slr", i - 1))
                    p.op(lambda e, i=i: e.transpose(ps_psT, pos_et[0:8, i * 128:(i + 1) * 128],
                                                    identf[0:8, 0:8]), inc=("t0", ("psT", i), 1))
            elif which == "gpsimd":
                p.wait("v0", ("zrow",))
                p.op(lambda e: e.dma_start(out=xn_tm_h[S:S + 1, :], in_=zrow16[0:1, :]),
                     inc=("gpm", ("zr",), 16))
                p.wait("v0", ("sent",))
                p.op(lambda e: e.dma_start(
                    out=perm_h.rearrange("(a b) one -> a (b one)", b=NE * NPAD // 128),
                    in_=sent_i[:, 0:NE * NPAD // 128]), inc=("gpm", ("pref",), 16))
                p.wait("gpm", ("pref",))
                for i in range(NT):
                    p.wait("v0", ("slot", i))
                    p.op(lambda e, i=i: e.indirect_dma_start(
                        out=perm_h, out_offset=IndirectOffsetOnAxis(slotub[:, i:i + 1], 0),
                        in_=tokid[:, i:i + 1], in_offset=None,
                        bounds_check=NE * NPAD - 1, oob_is_err=False),
                        inc=("gpm", ("psc", i), 16))
                p.wait("gpm", ("psc", NT - 1))

        wbf = [cv(AR1, 0, 512, BF16), cv(AR1, 512, 512, BF16),
               cv(AR1, 14352, 512, BF16), cv(AR1, 14864, 512, BF16)]
        wTs = [cv(AR1, 1024, 512, BF16), cv(AR1, 1536, 512, BF16)]
        xi_db = [cv(AR1, 2048, 4104), cv(AR1, 6152, 4104)]
        xc_a = cv(AR1, 10256, 4096)
        xc_b = cv(AR3, 0, 4096)
        xsz = [cv(AR3, 4096, 2048, BF16), cv(AR3, 6144, 2048, BF16)]
        zsz = [cv(AR3, 8192, 2048, BF16), cv(AR3, 10240, 2048, BF16)]
        wstg = [cv(AR3, 12288, 1024), cv(AR3, 13312, 1024)]
        ps_w = [PSA[:, 0:512].bitcast(BF16), PSA[:, 512:1024].bitcast(BF16)]
        ps_xz = [PSB[:, 0:512], PSB[:, 512:1024]]

        def p2(which):
            for mi in range(32):
                par = mi % 2
                is_xi = mi < 16
                if which == "sync":
                    if mi >= 2:
                        p.wait("v1", ("wcp", mi - 2))
                    p.op(lambda e, mi=mi: e.dma_start(out=wstg[mi % 2][:, :],
                                                      in_=inw_h[mi * 128:(mi + 1) * 128, :]),
                         inc=("sw0" if mi % 2 == 0 else "sw1", ("wst", mi), 16))
                    if mi == 31:
                        p.wait("sw0", ("wst", 30))
                        p.wait("sw1", ("wst", 31))
                if which == "gpsimd":
                    if is_xi:
                        p.wait("a1", ("xs", mi))
                        p.op(lambda e, mi=mi, par=par: e.dma_start(
                            out=xs_hbm[:, mi * S:(mi + 1) * S], in_=xsz[par][:, :]),
                            inc=("gxs0" if par == 0 else "gxs1", ("xsst", mi), 16))
                    else:
                        p.wait("a1", ("zev", mi, 7))
                        p.op(lambda e, mi=mi: e.dma_start(
                            out=sz_hbm[:, (mi - 16) * S:(mi - 15) * S], in_=zsz[(mi - 16) % 2][:, :]),
                            inc=("gsz0" if (mi - 16) % 2 == 0 else "gsz1", ("szst", mi), 16))
                    if mi == 31:
                        p.wait("gxs0", ("xsst", 14))
                        p.wait("gxs1", ("xsst", 15))
                        p.wait("gsz0", ("szst", 30))
                        p.wait("gsz1", ("szst", 31))
                elif which == "tensor":
                    p.wait("v1", ("wcp", mi))
                    if mi >= 2:
                        p.wait("v1", ("wev", mi - 2))
                    for j in range(8):
                        inc = ("t1", ("wT", mi), 1) if j == 7 else None
                        p.op(lambda e, par=par, j=j: e.transpose(
                            ps_w[par][:, j * 128:(j + 1) * 128], wbf[mi % 4][:, j * 128:(j + 1) * 128],
                            identb[:, :]), inc=inc)
                    p.wait("v1", ("wev", mi))
                    for c in range(8):
                        q = mi * 8 + c
                        if q >= 2:
                            mi2, c2 = divmod(q - 2, 8)
                            if mi2 < 16:
                                p.wait("v1", ("xzev", mi2, c2))
                            else:
                                p.wait("a1", ("zev", mi2, c2))
                        for j in range(8):
                            inc = ("t1", ("xz", mi, c), 1) if j == 7 else None
                            p.op(lambda e, par=par, j=j, c=c, q=q: e.matmul(
                                ps_xz[q % 2][:, :], wTs[par][:, j * 128:(j + 1) * 128],
                                xn2_sb[:, j * S + c * 512:j * S + (c + 1) * 512],
                                start=(j == 0), stop=(j == 7)), inc=inc)
                elif which == "vector":
                    def wcp(m):
                        p.wait("sw0" if m % 2 == 0 else "sw1", ("wst", m))
                        if m >= 4:
                            p.wait("t1", ("wT", m - 4))
                        p.op(lambda e, m=m: e.tensor_copy(wbf[m % 4][:, :], wstg[m % 2][:, :]),
                             inc=("v1", ("wcp", m), 1))
                    if mi == 0:
                        wcp(0)
                        wcp(1)
                    if mi + 2 < 32:
                        wcp(mi + 2)
                    p.wait("t1", ("wT", mi))
                    p.op(lambda e, par=par: e.tensor_copy(wTs[par][:, :], ps_w[par][:, :]),
                         inc=("v1", ("wev", mi), 1))
                    if is_xi:
                        p.op(lambda e, par=par: e.memset(xi_db[par][:, 0:3], 0.0))
                        for c in range(8):
                            q = mi * 8 + c
                            p.wait("t1", ("xz", mi, c))
                            p.op(lambda e, par=par, c=c, q=q: e.tensor_copy(
                                xi_db[par][:, 3 + c * 512:3 + (c + 1) * 512], ps_xz[q % 2][:, :]),
                                inc=("v1", ("xzev", mi, c), 1))
                        p.wait("v1", ("xzev", mi, 7))
                        c0t, c1t, c2t = p.vtag(), p.vtag(), p.vtag()
                        p.op(lambda e, par=par, mi=mi: e.tensor_scalar(
                            xc_a[:, :], xi_db[par][:, 0:S], convw_col[:, mi * 4:mi * 4 + 1], None,
                            op0=ALU.mult), inc=("vself", c0t, 1))
                        p.wait("vself", c0t)
                        if mi >= 1:
                            p.wait("a1", ("xs", mi - 1))
                        p.op(lambda e, par=par, mi=mi: e.scalar_tensor_tensor(
                            xc_b[:, :], xi_db[par][:, 1:1 + S], convw_col[:, mi * 4 + 1:mi * 4 + 2],
                            xc_a[:, :], op0=ALU.mult, op1=ALU.add), inc=("vself", c1t, 1))
                        p.wait("vself", c1t)
                        p.op(lambda e, par=par, mi=mi: e.scalar_tensor_tensor(
                            xc_a[:, :], xi_db[par][:, 2:2 + S], convw_col[:, mi * 4 + 2:mi * 4 + 3],
                            xc_b[:, :], op0=ALU.mult, op1=ALU.add), inc=("vself", c2t, 1))
                        p.wait("vself", c2t)
                        p.op(lambda e, par=par, mi=mi: e.scalar_tensor_tensor(
                            xc_b[:, :], xi_db[par][:, 3:3 + S], convw_col[:, mi * 4 + 3:mi * 4 + 4],
                            xc_a[:, :], op0=ALU.mult, op1=ALU.add), inc=("v1", ("conv", mi), 1))
                elif which == "scalar":
                    if is_xi:
                        p.wait("v1", ("conv", mi))
                        if mi >= 2:
                            p.wait("gxs0" if par == 0 else "gxs1", ("xsst", mi - 2))
                        p.op(lambda e, par=par, mi=mi: e.activation(
                            xsz[par][:, :], xc_b[:, :], AF.Silu, bias=convb_col[:, mi:mi + 1]),
                            inc=("a1", ("xs", mi), 1))
                    else:
                        zp = (mi - 16) % 2
                        if mi >= 18:
                            p.wait("gsz0" if zp == 0 else "gsz1", ("szst", mi - 2))
                        for c in range(8):
                            p.wait("t1", ("xz", mi, c))
                            p.op(lambda e, zp=zp, c=c, q=mi * 8 + c: e.activation(
                                zsz[zp][:, c * 512:(c + 1) * 512], ps_xz[q % 2][:, :], AF.Silu),
                                inc=("a1", ("zev", mi, c), 1))

        xpwst = cv(AR1, 0, 2048)
        xp_sb = cv(AR1, 2048, 4096)
        xpw_bf = cv(AR1, 6144, 1024, BF16)
        xprojT = cv(AR1, 7168, 272, BF16)
        xst = [cv(AR3, 0, 4096, BF16), cv(AR3, 4096, 4096, BF16)]
        ps_xpT = PSA[:, 0:272].bitcast(BF16)
        ps_xp = PSB[:, 0:512]

        def p3(which):
            if which == "sync":
                p.op(lambda e: e.dma_start(out=xpwst[0:33, :], in_=xpw_h[:, :]), inc=("sstg", "xpw", 16))
                for c in range(8):
                    if c >= 2:
                        p.wait("v3", ("xpev", c - 2))
                    for kt in range(16):
                        sm = "sxst0" if c % 2 == 0 else "sxst1"
                        p.op(lambda e, c=c, kt=kt: e.dma_start(
                            out=xst[c % 2][:, kt * 512:(kt + 1) * 512],
                            in_=xs_hbm[:, kt * S + c * 512:kt * S + (c + 1) * 512]),
                            inc=(sm, ("xst", c, kt), 16))
                p.wait("sxst0", ("xst", 6, 15))
                p.wait("sxst1", ("xst", 7, 15))
            elif which == "vector":
                p.wait("sstg", "xpw")
                p.op(lambda e: e.tensor_copy(xpw_bf[0:33, :], xpwst[0:33, :]), inc=("v3", "cast", 1))
                p.wait("t3", "xpT")
                p.op(lambda e: e.tensor_copy(xprojT[:, :], ps_xpT[:, 0:544]), inc=("v3", "Tev", 1))
                for c in range(8):
                    p.wait("t3", ("xp", c))
                    p.op(lambda e, c=c: e.tensor_copy(xp_sb[0:33, c * 512:(c + 1) * 512], ps_xp[0:33, :]),
                         inc=("v3", ("xpev", c), 1))
            elif which == "tensor":
                p.wait("v3", "cast")
                for kt in range(16):
                    inc = ("t3", "xpT", 1) if kt == 15 else None
                    p.op(lambda e, kt=kt: e.transpose(ps_xpT[:, kt * 34:kt * 34 + 33],
                                                      xpw_bf[0:33, kt * 128:(kt + 1) * 128], identb[0:33, 0:33]),
                         inc=inc)
                p.wait("v3", "Tev")
                for c in range(8):
                    if c >= 1:
                        p.wait("v3", ("xpev", c - 1))
                    p.wait("sxst0" if c % 2 == 0 else "sxst1", ("xst", c, 15))
                    for kt in range(16):
                        inc = ("t3", ("xp", c), 1) if kt == 15 else None
                        p.op(lambda e, c=c, kt=kt: e.matmul(ps_xp[0:33, :],
                                                            xprojT[:, kt * 34:kt * 34 + 33],
                                                            xst[c % 2][:, kt * 512:(kt + 1) * 512],
                                                            start=(kt == 0), stop=(kt == 15)), inc=inc)
            elif which == "gpsimd":
                p.wait("v3", ("xpev", 7))
                p.op(lambda e: e.dma_start(out=drrow_h[:, :], in_=xp_sb[0:1, :]), inc=("g3", "dr", 16))
                for tc in range(NTC):
                    p.op(lambda e, tc=tc: e.dma_start(
                        out=bbc_h[tc * 16:(tc + 1) * 16, :],
                        in_=xp_sb[1:17, tc * TCH:(tc + 1) * TCH]), inc=("g3", ("bbc", tc), 16))
                    p.op(lambda e, tc=tc: e.dma_start(
                        out=cbc_h[tc * 16:(tc + 1) * 16, :],
                        in_=xp_sb[17:33, tc * TCH:(tc + 1) * TCH]), inc=("g3", ("cbc", tc), 16))
                p.wait("g3", ("cbc", NTC - 1))

        Bq = [cv(AR1, 0, 2048, BF16), cv(AR1, 2048, 2048, BF16)]
        Cq = [cv(AR1, 4096, 2048, BF16), cv(AR1, 6144, 2048, BF16)]
        dA3 = [cv(AR1, 8192 + 512 * r, 512, BF16) for r in range(5)]
        idx16 = cv(AR1, 10752, WRC // 2).bitcast(I16)
        x02 = [cv(AR1, 11264, 512, BF16), cv(AR1, 11776, 512, BF16)]
        ab2 = [cv(AR1, 12288, 512, BF16), cv(AR1, 12800, 512, BF16)]
        delta2 = [cv(AR1, 13312, 512, BF16), cv(AR1, 13824, 512, BF16)]
        drq = [cv(AR1, 14336, 512, BF16), cv(AR1, 14848, 512, BF16)]
        e_sb = cv(AR1, 15360, 512, BF16)
        oacc16 = cv(AR2, 0, NG * 1024, BF16)
        w1s2 = [cv(AR2, 6144, 4096, BF16), cv(AR2, 10240, 4096, BF16)]
        hm2 = [cv(AR2, 14336, 512, BF16), cv(AR2, 14848, 512, BF16)]
        xgT_b = cv(AR2, 15360, NPAD * 4, BF16)
        w2s2 = [cv(AR3, 0, 4096, BF16), cv(AR3, 4096, 4096, BF16)]
        u_b2 = [cv(AR3, 8192, 512, BF16), cv(AR3, 8704, 512, BF16)]
        xs_t2 = [cv(AR3, 9216, 512, BF16), cv(AR3, 9728, 512, BF16)]
        sz_t2 = [cv(AR3, 10240, 512, BF16), cv(AR3, 10752, 512, BF16)]
        y2b2 = [cv(AR3, 11264, 512, BF16), cv(AR3, 11776, 512, BF16)]
        h2 = [cv(AR3, 12288, 512, BF16), cv(AR3, 12800, 512, BF16)]
        dbx3 = [cv(AR3, 13312, 512, BF16), cv(AR3, 13824, 512, BF16), cv(AR3, 14336, 512, BF16)]
        prod2 = [cv(AR3, 14848, 512, BF16), cv(AR3, 15360, 512, BF16)]
        y_acc2 = cv(AR3, 15872, 512, BF16)
        ps_h = [PSA, PSB]
        ps_o = [PSC, PSD]

        NSEG = NTC * NDB

        def seg_of(si):
            return si // NDB, si % NDB

        def scan_sync_seg(si):
            tc, db = seg_of(si)
            if si == 0:
                for tcn in range(2):
                    p.op(lambda e, tcn=tcn: e.dma_start(
                        out=drq[tcn][:, :],
                        in_=drrow_h[0:1, tcn * TCH:(tcn + 1) * TCH]
                        .partition_broadcast(128).squeeze(1)),
                        inc=("sdr0" if tcn % 2 == 0 else "sdr1", ("dr", tcn), 16))
            par = si % 2
            if si >= 2:
                p.wait("v4", ("y2", si - 2))
            sem = "ssg0" if par == 0 else "ssg1"
            p.op(lambda e, tc=tc, db=db: e.dma_start(
                out=xs_t2[par][:, :], in_=xs_hbm[:, db * S + tc * TCH:db * S + (tc + 1) * TCH]),
                inc=(sem, ("xs", si), 16))
            p.op(lambda e, tc=tc, db=db: e.dma_start(
                out=sz_t2[par][:, :], in_=sz_hbm[:, db * S + tc * TCH:db * S + (tc + 1) * TCH]),
                inc=(sem, ("sg", si), 16))
            if si >= 1:
                ptc, pdb = seg_of(si - 1)
                p.wait("v4", ("y2", si - 1))
                p.op(lambda e, ptc=ptc, pdb=pdb, si=si: e.dma_start(
                    out=y2_hbm[:, pdb * S + ptc * TCH:pdb * S + (ptc + 1) * TCH],
                    in_=y2b2[(si - 1) % 2][:, :]),
                    inc=("gy0" if (si - 1) % 2 == 0 else "gy1", ("y2st", si - 1), 16))

            def bc_load(qn, which_bc):
                gq = si * 4 + qn
                h = cbc_h if which_bc == "C" else bbc_h
                sems = ("scq0", "scq1") if which_bc == "C" else ("sbq0", "sbq1")
                if gq >= 2:
                    psi, pqn = divmod(gq - 2, 4)
                    if which_bc == "C":
                        p.wait("v4", ("prodq", psi, pqn))
                    else:
                        p.wait("vself", ("bx", psi, pqn * 4 + 3))
                dst = (Cq if which_bc == "C" else Bq)[qn % 2]
                p.op(lambda e, h=h, dst=dst, tc=tc, qn=qn: e.dma_start(
                    out=dst[:, :],
                    in_=h.rearrange("(a r) t -> a (r t)", a=1)
                    [0:1, tc * 16 * TCH + qn * 4 * TCH:tc * 16 * TCH + (qn + 1) * 4 * TCH]
                    .partition_broadcast(128).squeeze(1)),
                    inc=(sems[qn % 2], (which_bc, si, qn), 16))

            for qn in range(4):
                bc_load(qn, "B")
                bc_load(qn, "C")
            if si % NDB == NDB - 1 and si // NDB + 2 < NTC:
                tcn = si // NDB + 2
                p.wait("v4", ("x0", (tcn - 2) * NDB + NDB - 1))
                p.op(lambda e, tcn=tcn: e.dma_start(
                    out=drq[tcn % 2][:, :],
                    in_=drrow_h[0:1, tcn * TCH:(tcn + 1) * TCH]
                    .partition_broadcast(128).squeeze(1)),
                    inc=("sdr0" if tcn % 2 == 0 else "sdr1", ("dr", tcn), 16))

        def prep_x0ab(sj):
            tcj, dbj = seg_of(sj)
            def f():
                p.wait("sdr0" if tcj % 2 == 0 else "sdr1", ("dr", tcj))
                if sj >= 2:
                    p.wait("a4", ("ln", sj - 2))
                p.op(lambda e, tcj=tcj, dbj=dbj, sj=sj: e.tensor_scalar(
                    x02[sj % 2][:, :], drq[tcj % 2][:, :], dtw_col[:, dbj:dbj + 1],
                    dtb_col[:, dbj:dbj + 1], op0=ALU.mult, op1=ALU.add),
                    inc=("v4", ("x0", sj), 1))
                p.wait("v4", ("x0", sj))
                p.op(lambda e, sj=sj: e.scalar_tensor_tensor(
                    ab2[sj % 2][:, :], x02[sj % 2][:, :], cM1, x02[sj % 2][:, :],
                    op0=ALU.mult, op1=ALU.max), inc=("v4", ("ab", sj), 1))
            return f

        def scan_v_seg_ops(si):
            tc, db = seg_of(si)
            par = si % 2
            sem = "ssg0" if par == 0 else "ssg1"
            ops = []

            def first():
                p.wait(sem, ("sg", si))
            ops.append(first)

            if si + 1 < NSEG:
                ops.append(prep_x0ab(si + 1))
            else:
                ops.append(lambda: None)

            def prep_delta():
                p.wait("a4", ("ln", si))
                if si >= 2:
                    p.wait("a4", ("dAx", si - 2, DS - 1))
                p.op(lambda e, si=si: e.scalar_tensor_tensor(
                    delta2[si % 2][:, :], x02[si % 2][:, :], zcol, ab2[si % 2][:, :],
                    op0=ALU.max, op1=ALU.add),
                    inc=("v4", ("d", si), 1))
            ops.append(prep_delta)

            def prep_u():
                p.wait("v4", ("d", si))
                p.op(lambda e, si=si, par=par: e.tensor_tensor(
                    u_b2[par][:, :], delta2[si % 2][:, :], xs_t2[par][:, :], op=ALU.mult),
                    inc=("v4", ("uw", si), 1))
            ops.append(prep_u)

            def em_dbx(k):
                def f():
                    if k == 0:
                        p.wait("v4", ("uw", si))
                    p.wait("sbq0" if (k // 4) % 2 == 0 else "sbq1", ("B", si, k // 4))
                    g = si * DS + k
                    if g >= 3:
                        ptc2, r2 = divmod(g - 3, NDB * DS)
                        pdb2, pn2 = divmod(r2, DS)
                        p.wait("v4", ("scan", ptc2, pdb2, pn2))
                    p.op(lambda e, k=k: e.tensor_tensor(
                        dbx3[k % 3][:, :], u_b2[par][:, :],
                        Bq[(k // 4) % 2][:, (k % 4) * TCH:(k % 4 + 1) * TCH],
                        op=ALU.mult), inc=("vself", ("bx", si, k), 1))
                ops.append(f)

            def em_scan(k):
                def f():
                    p.wait("a4", ("dAx", si, k))
                    p.wait("vself", ("bx", si, k))
                    if k >= 2:
                        p.wait("vself", ("cy", si, k - 2))
                    p.op(lambda e, k=k, db=db: e.tensor_tensor_scan(
                        h2[k % 2][:, :], dA3[k % 5][:, :], dbx3[k % 3][:, :],
                        carry[:, db * DS + k:db * DS + k + 1], op0=ALU.mult, op1=ALU.add),
                        inc=("v4", ("scan", tc, db, k), 1))
                ops.append(f)

            def prtag(k):
                if k % 4 == 3:
                    return ("v4", ("prodq", si, k // 4))
                return ("vself", ("pr", si, k))

            def em_prod(k):
                def f():
                    p.wait("scq0" if (k // 4) % 2 == 0 else "scq1", ("C", si, k // 4))
                    p.wait("v4", ("scan", tc, db, k))
                    if k >= 2:
                        p.wait("vself", ("ya", si, k - 2))
                    elif si >= 1 and k == 1:
                        p.wait("vself", ("ya", si - 1, DS - 1))
                    elif si >= 1 and k == 0:
                        p.wait("v4", ("y2", si - 1))
                    sm, tag = prtag(k)
                    p.op(lambda e, k=k: e.tensor_tensor(
                        prod2[k % 2][:, :], h2[k % 2][:, :],
                        Cq[(k // 4) % 2][:, (k % 4) * TCH:(k % 4 + 1) * TCH], op=ALU.mult),
                        inc=(sm, tag, 1))
                ops.append(f)

            def em_add(k):
                def f():
                    sm, tag = prtag(k)
                    p.wait(sm, tag)
                    if k >= 1:
                        p.wait("vself", ("ya", si, k - 1))
                    if k == 0:
                        if si >= 2:
                            p.wait("vself", ("yt", si - 2))
                        p.op(lambda e: e.tensor_copy(y_acc2[:, :], prod2[0][:, :]),
                             inc=("vself", ("ya", si, 0), 1))
                    else:
                        p.op(lambda e, k=k: e.tensor_tensor(y_acc2[:, :], y_acc2[:, :],
                                                            prod2[k % 2][:, :], op=ALU.add),
                             inc=("vself", ("ya", si, k), 1))
                ops.append(f)

            def em_carry(k):
                def f():
                    p.op(lambda e, k=k, db=db: e.tensor_copy(
                        carry[:, db * DS + k:db * DS + k + 1], h2[k % 2][:, 1023:1024]),
                        inc=("vself", ("cy", si, k), 1))
                ops.append(f)

            em_dbx(0)
            em_dbx(1)
            for k in range(DS):
                if k + 2 < DS:
                    em_dbx(k + 2)
                em_scan(k)
                if k >= 1:
                    em_prod(k - 1)
                    em_carry(k - 1)
                if k >= 2:
                    em_add(k - 2)
            em_prod(DS - 1)
            em_carry(DS - 1)
            em_add(DS - 2)
            em_add(DS - 1)

            def fin1():
                p.wait("vself", ("ya", si, DS - 1))
                p.op(lambda e, db=db: e.scalar_tensor_tensor(
                    prod2[0][:, :], xs_t2[par][:, :], d_col[:, db:db + 1], y_acc2[:, :],
                    op0=ALU.mult, op1=ALU.add), inc=("vself", ("yt", si), 1))
            ops.append(fin1)

            def fin2():
                p.wait("vself", ("yt", si))
                if si >= 2:
                    p.wait("gy0" if par == 0 else "gy1", ("y2st", si - 2))
                p.op(lambda e: e.tensor_tensor(y2b2[par][:, :], prod2[0][:, :], sz_t2[par][:, :],
                                               op=ALU.mult),
                     inc=("v4", ("y2", si), 1))
            ops.append(fin2)
            return ops

        def ffn_emit_w(e, fb):
            fbg = e * 4 + fb
            fpar = fbg % 2
            gsem = "g60" if fpar == 0 else "g61"
            if fbg >= 2:
                pe_, pf = divmod(fbg - 2, 4)
                p.wait("t6", ("o2", pe_ * HPE + pf * NG + NG - 1))
            if fb == 0:
                p.op(lambda e2, e=e: e2.dma_start(out=b2cur[e % 2][:, :],
                                                  in_=b2_h[0:1, e * D:(e + 1) * D]),
                     inc=(gsem, ("b2", e), 16))
            for kt in range(8):
                p.op(lambda e2, e=e, fb=fb, kt=kt, fpar=fpar: e2.dma_start(
                    out=w1s2[fpar][:, kt * 1024:(kt + 1) * 1024],
                    in_=w1_h[e, kt * 128:(kt + 1) * 128, fb * 1024:(fb + 1) * 1024]),
                    inc=(gsem, ("w1", fbg, kt), 16))
            for ft in range(8):
                inc = (gsem, ("w", fbg), 16) if ft == 7 else (gsem, ("w2", fbg, ft), 16)
                p.op(lambda e2, e=e, fb=fb, ft=ft, fpar=fpar: e2.dma_start(
                    out=w2s2[fpar][:, ft * 1024:(ft + 1) * 1024],
                    in_=w2_h[e, fb * 1024 + ft * 128:fb * 1024 + (ft + 1) * 128, :]),
                    inc=inc)

        def ffn_gp():
            for r in range(8):
                p.op(lambda e2, r=r: e2.dma_start(
                    out=idx16[r * 16:(r + 1) * 16, :],
                    in_=perm_h.rearrange("(q c) one -> q (c one)", q=16)),
                    inc=("gidx", ("idx", r), 16))
            p.wait("gidx", ("idx", 7))
            def emit_pair_scatter(m):
                p.wait("v6", ("ev", (2 * m + 2) * HPE - 1))
                p.op(lambda e2, m=m: e2.dma_scatter_add(
                    out_ap=acc_hbm[:, :],
                    in_ap=oacc16.rearrange("q (b d) -> q b d", b=2 * NG),
                    idxs_ap=idx16[:, 2 * m * (NPAD // 16):(2 * m + 2) * (NPAD // 16)],
                    num_idxs=2 * NPAD, num_idxs_reg=2 * NPAD, elem_size=D),
                    inc=("gsc", ("sc", m), 16))

            for e in range(NE):
                if e >= 1:
                    p.wait("t6", ("hb", e * HPE - 1))
                p.op(lambda e2, e=e: e2.dma_gather(
                    out_ap=xgT_b.rearrange("q (c t) -> q c t", c=8),
                    in_ap=xn_tm_h[:, :],
                    idxs_ap=idx16[:, e * (NPAD // 16):(e + 1) * (NPAD // 16)],
                    num_idxs=NPAD, num_idxs_reg=NPAD, elem_size=D,
                    transpose=True), inc=("ggx0", ("xg", e), 16))
                ffn_emit_w(e, 0)
                ffn_emit_w(e, 1)
                if e >= 2 and e % 2 == 0:
                    emit_pair_scatter(e // 2 - 1)
                ffn_emit_w(e, 2)
                ffn_emit_w(e, 3)
            emit_pair_scatter(NE // 2 - 1)
            p.wait("gsc", ("sc", NE // 2 - 1))
            p.wait("ggx0", ("xg", NE - 1))
            p.wait("g60", ("w", 4 * NE - 2))
            p.wait("g61", ("w", 4 * NE - 1))

        def ffn_pe():
            for e in range(NE):
                for fb in range(4):
                    fbg = e * 4 + fb
                    p.wait("g60" if fbg % 2 == 0 else "g61", ("w", fbg))
                    if fb == 0:
                        p.wait("ggx0", ("xg", e))
                    for g in range(NG):
                        hseq = e * HPE + fb * NG + g
                        hpar = hseq % 2
                        if hseq >= 2:
                            p.wait("a6", ("gel", hseq - 2))
                        nmm = 0
                        for ft in range(8):
                            for kt in range(8):
                                nmm += 1
                                inc = ("t6", ("hb", hseq), 1) if nmm == 64 else None
                                p.op(lambda e2, hpar=hpar, fbg=fbg, ft=ft, kt=kt, g=g: e2.matmul(
                                    ps_h[hpar][:, ft * 128:(ft + 1) * 128],
                                    w1s2[fbg % 2][:, kt * 1024 + ft * 128:kt * 1024 + ft * 128 + 128],
                                    xgT_b[:, kt * NPAD + g * 128:kt * NPAD + (g + 1) * 128],
                                    start=(kt == 0), stop=(kt == 7)), inc=inc)
                        p.wait("a6", ("gel", hseq))
                        if hseq >= 2:
                            p.wait("v6", ("ev", hseq - 2))
                        nmm = 0
                        tot = 16 + (2 if fb == 0 else 0)
                        if fb == 0:
                            for dmh in range(2):
                                nmm += 1
                                p.op(lambda e2, hpar=hpar, dmh=dmh, e=e: e2.matmul(
                                    ps_o[hpar][:, dmh * 512:(dmh + 1) * 512], ones1[:, :],
                                    b2cur[e % 2][:, dmh * 512:(dmh + 1) * 512],
                                    start=True, stop=False))
                        for ft in range(8):
                            for dmh in range(2):
                                nmm += 1
                                inc = ("t6", ("o2", hseq), 1) if nmm == tot else None
                                p.op(lambda e2, hpar=hpar, fbg=fbg, ft=ft, dmh=dmh, fb=fb: e2.matmul(
                                    ps_o[hpar][:, dmh * 512:(dmh + 1) * 512],
                                    hm2[hpar][:, ft * 128:(ft + 1) * 128],
                                    w2s2[fbg % 2][:, ft * 1024 + dmh * 512:ft * 1024 + dmh * 512 + 512],
                                    start=(fb != 0 and ft == 0), stop=(ft == 7)), inc=inc)

        def ffn_act_hseq(hseq):
            e, r = divmod(hseq, HPE)
            fb, g = divmod(r, NG)
            hpar = hseq % 2
            p.wait("t6", ("hb", hseq))
            if hseq >= 2:
                p.wait("t6", ("o2", hseq - 2))
            for ft in range(8):
                inc = ("a6", ("gel", hseq), 1) if ft == 7 else None
                p.op(lambda e2, hpar=hpar, ft=ft, e=e, fb=fb: e2.activation(
                    hm2[hpar][:, ft * 128:(ft + 1) * 128],
                    ps_h[hpar][:, ft * 128:(ft + 1) * 128], AF.Gelu,
                    bias=b1_col[:, e * 32 + fb * 8 + ft:e * 32 + fb * 8 + ft + 1]), inc=inc)

        def act_expln(sj):
            p.wait("v4", ("ab", sj))
            p.op(lambda e, sj=sj: e.activation(e_sb[:, :], ab2[sj % 2][:, :], AF.Exp, scale=cM1),
                 inc=("aself", ("e4", sj), 1))
            p.wait("aself", ("e4", sj))
            p.op(lambda e, sj=sj: e.activation(ab2[sj % 2][:, :], e_sb[:, :], AF.Ln, bias=cONE),
                 inc=("a4", ("ln", sj), 1))

        def scan_act(si):
            p.wait("v4", ("d", si))
            for n in range(DS):
                gn = si * DS + n
                if gn >= 5:
                    ptc2, r2 = divmod(gn - 5, NDB * DS)
                    pdb2, pn2 = divmod(r2, DS)
                    p.wait("v4", ("scan", ptc2, pdb2, pn2))
                p.op(lambda e, si=si, n=n: e.activation(
                    dA3[n % 5][:, :], delta2[si % 2][:, :], AF.Exp, scale=a_col[:, n:n + 1]),
                    inc=("a4", ("dAx", si, n), 1))
            if si + 1 < NSEG:
                act_expln(si + 1)

        def ffn_v_thunks():
            ops = []
            for e in range(NE):
                for fb in range(4):
                    for g in range(NG):
                        hseq = e * HPE + fb * NG + g
                        gh = (e % 2) * NG + g
                        def ev(hseq=hseq, e=e, fb=fb, gh=gh):
                            p.wait("t6", ("o2", hseq))
                            if fb == 0:
                                if e >= 2:
                                    p.wait("gsc", ("sc", e // 2 - 1))
                                p.op(lambda e2, gh=gh, hseq=hseq: e2.tensor_copy(
                                    oacc16[:, gh * 1024:(gh + 1) * 1024], ps_o[hseq % 2][:, :]),
                                    inc=("v6", ("ev", hseq), 1))
                            else:
                                p.wait("v6", ("ev", hseq - NG))
                                p.op(lambda e2, gh=gh, hseq=hseq: e2.tensor_tensor(
                                    oacc16[:, gh * 1024:(gh + 1) * 1024], ps_o[hseq % 2][:, :],
                                    oacc16[:, gh * 1024:(gh + 1) * 1024], op=ALU.add),
                                    inc=("v6", ("ev", hseq), 1))
                        ops.append(ev)
            return ops

        def p46(which):
            if which == "sync":
                for si in range(NSEG):
                    scan_sync_seg(si)
                ptc, pdb = seg_of(NSEG - 1)
                p.wait("v4", ("y2", NSEG - 1))
                p.op(lambda e, ptc=ptc, pdb=pdb: e.dma_start(
                    out=y2_hbm[:, pdb * S + ptc * TCH:pdb * S + (ptc + 1) * TCH],
                    in_=y2b2[(NSEG - 1) % 2][:, :]),
                    inc=("gy1", ("y2st", NSEG - 1), 16))
                p.wait("gy0", ("y2st", NSEG - 2))
                p.wait("gy1", ("y2st", NSEG - 1))
                p.wait("ssg0", ("sg", NSEG - 2))
                p.wait("ssg1", ("sg", NSEG - 1))
                p.wait("scq0", ("C", NSEG - 1, 2))
                p.wait("scq1", ("C", NSEG - 1, 3))
                p.wait("sbq0", ("B", NSEG - 1, 2))
                p.wait("sbq1", ("B", NSEG - 1, 3))
                p.wait("sdr0", ("dr", 2))
                p.wait("sdr1", ("dr", 3))
            elif which == "gpsimd":
                ffn_gp()
            elif which == "vector":
                EV = ffn_v_thunks()
                prep_x0ab(0)()
                for si in range(NSEG):
                    seg = scan_v_seg_ops(si)
                    for opf in seg[:23]:
                        opf()
                    if si >= 1:
                        EV[(si - 1) * 3]()
                        EV[(si - 1) * 3 + 1]()
                    for opf in seg[23:28]:
                        opf()
                    if si >= 1:
                        EV[(si - 1) * 3 + 2]()
                    for opf in seg[28:]:
                        opf()
                for j in range(3):
                    EV[(NSEG - 1) * 3 + j]()
            elif which == "scalar":
                act_expln(0)
                for si in range(NSEG):
                    scan_act(si)
                    for j in range(3):
                        ffn_act_hseq(si * 3 + j)
            elif which == "tensor":
                ffn_pe()

        owT = cv(AR1, 0, 8192, BF16)
        y2sb = [cv(AR1, 8192, 4096, BF16), cv(AR3, 0, 4096, BF16), cv(AR3, 4096, 4096, BF16)]
        owstg = [cv(AR3, 8192, 2048), cv(AR3, 10240, 2048)]
        owbf = [cv(AR2, 0, 1024, BF16), cv(AR2, 1024, 1024, BF16)]
        acc_t = [cv(AR2, 2048, 512, BF16), cv(AR2, 2560, 512, BF16)]
        outb = [cv(AR2, 4096, 1024), cv(AR2, 5120, 1024)]
        ps_out = [PSA, PSB]
        ps_ow = PSC[:, :].bitcast(BF16)

        def p5(which):
            if which == "gpsimd":
                for g in range(32):
                    p.wait("v5", ("opev", g))
                    p.op(lambda e, g=g: e.dma_start(out=out_h[g * 128:(g + 1) * 128, :],
                                                    in_=outb[g % 2][:, :]),
                         inc=("sout0" if g % 2 == 0 else "sout1", ("o5", g), 16))
                p.wait("sout0", ("o5", 30))
                p.wait("sout1", ("o5", 31))
            elif which == "tensor":
                for mr in range(8):
                    p.wait("v5", ("owcp", mr))
                    if mr >= 1:
                        p.wait("v5", ("owev", mr - 1))
                    for kt in range(16):
                        inc = ("t5", ("owT", mr), 1) if kt == 15 else None
                        p.op(lambda e, mr=mr, kt=kt: e.transpose(
                            ps_ow[:, kt * 128:(kt + 1) * 128], owbf[mr % 2][:, kt * 128:(kt + 1) * 128],
                            identb[:, :]), inc=inc)
                p.wait("v5", ("owev", 7))
                for c in range(8):
                    for tt in range(4):
                        g = c * 4 + tt
                        p.wait("sy2%d" % (c % 3), ("y2", c, 15))
                        if g >= 2:
                            p.wait("v5", ("opev", g - 2))
                        nmm = 0
                        for kt in range(16):
                            for hh in range(2):
                                nmm += 1
                                inc = ("t5", ("op", g), 1) if nmm == 32 else None
                                p.op(lambda e, c=c, tt=tt, g=g, kt=kt, hh=hh: e.matmul(
                                    ps_out[g % 2][:, hh * 512:(hh + 1) * 512],
                                    y2sb[c % 3][:, kt * 512 + tt * 128:kt * 512 + (tt + 1) * 128],
                                    owT[:, kt * 1024 + hh * 512:kt * 1024 + (hh + 1) * 512],
                                    start=(kt == 0), stop=(kt == 15)), inc=inc)
            elif which == "sync":
                for c in range(8):
                    if c >= 3:
                        p.wait("t5", ("op", (c - 3) * 4 + 3))
                    for kt in range(16):
                        p.op(lambda e, c=c, kt=kt: e.dma_start(
                            out=y2sb[c % 3][:, kt * 512:(kt + 1) * 512],
                            in_=y2_hbm[:, kt * S + c * 512:kt * S + (c + 1) * 512]),
                            inc=("sy2%d" % (c % 3), ("y2", c, kt), 16))
                    if c == 0:
                        for mr in range(8):
                            if mr >= 2:
                                p.wait("v5", ("owcp", mr - 2))
                            p.op(lambda e, mr=mr: e.dma_start(
                                out=owstg[mr % 2][:, :],
                                in_=outw_h[mr * 128:(mr + 1) * 128, :]),
                                inc=("sw0" if mr % 2 == 0 else "sw1", ("owst", mr), 16))
                    for tt in range(4):
                        g = c * 4 + tt
                        if g >= 2:
                            p.wait("v5", ("opev", g - 2))
                        p.op(lambda e, g=g: e.dma_start(out=acc_t[g % 2][:, :],
                                                        in_=acc_hbm[g * 128:(g + 1) * 128, :]),
                             inc=("sal0" if g % 2 == 0 else "sal1", ("acc", g), 16))
                p.wait("sal0", ("acc", 30))
                p.wait("sal1", ("acc", 31))
            elif which == "vector":
                def owcp(m):
                    p.wait("sw0" if m % 2 == 0 else "sw1", ("owst", m))
                    if m >= 2:
                        p.wait("t5", ("owT", m - 2))
                    p.op(lambda e, m=m: e.tensor_copy(owbf[m % 2][:, :], owstg[m % 2][:, :]),
                         inc=("v5", ("owcp", m), 1))
                owcp(0)
                owcp(1)
                for mr in range(8):
                    if mr + 2 < 8:
                        owcp(mr + 2)
                    p.wait("t5", ("owT", mr))
                    p.op(lambda e, mr=mr: e.tensor_copy(
                        owT.rearrange("z (kt m) -> z kt m", kt=16)[:, :, mr * 128:(mr + 1) * 128],
                        ps_ow.rearrange("z (kt m) -> z kt m", kt=16)),
                        inc=("v5", ("owev", mr), 1))
                for c in range(8):
                    for tt in range(4):
                        g = c * 4 + tt
                        p.wait("t5", ("op", g))
                        p.wait("sal0" if g % 2 == 0 else "sal1", ("acc", g))
                        if g >= 2:
                            p.wait("sout0" if g % 2 == 0 else "sout1", ("o5", g - 2))
                        p.op(lambda e, g=g: e.tensor_tensor(outb[g % 2][:, :], ps_out[g % 2][:, :],
                                                            acc_t[g % 2][:, :], op=ALU.add),
                             inc=("v5", ("opev", g), 1))


        DBG.update(dict(oh_tok=oh_tok[:, :], xn2_sb=xn2_sb[:, :], carry=carry[:, :],
                        a_col=a_col[:, :], b1_col=b1_col[:, :], oh_full=oh_tok,
                        idx16=idx16, Bq0=Bq[0], Bq1=Bq[1], Cq0=Cq[0], Cq1=Cq[1],
                        delta20=delta2[0], delta21=delta2[1], drq0=drq[0], drq1=drq[1],
                        x0t=x02[0], abt=ab2[0], e_sb=e_sb, dA30=dA3[0], oacc16=oacc16,
                        xgT_b=xgT_b, bbc_h=bbc_h, cbc_h=cbc_h))

        def engine_prog(which):
            plist = [p0, p1, p2, p3, p46, p5]
            for k in range(min(phases, 6)):
                plist[k](which)
                if k < 5:
                    p.barrier(k + 1)

        p.start_count()
        for w in ["sync", "gpsimd", "vector", "scalar", "tensor"]:
            engine_prog(w)

        semhandles = {s: es.enter_context(nc.semaphore(s)) for s in SEMS}
        p.start_emit(semhandles)
        blk = es.enter_context(nc.Block())

        @blk.sync
        def _(e):
            p.eng = e
            engine_prog("sync")

        @blk.gpsimd
        def _(e):
            p.eng = e
            engine_prog("gpsimd")

        @blk.vector
        def _(e):
            p.eng = e
            engine_prog("vector")

        @blk.scalar
        def _(e):
            p.eng = e
            engine_prog("scalar")

        @blk.tensor
        def _(e):
            p.eng = e
            engine_prog("tensor")

    bacc.Bacc.insert_library_loads(nc)
    mybir.codegen_inst_isa_subclasses(nc)
    return nc


def kernel(**inputs):
    gi = lambda k: np.ascontiguousarray(np.asarray(inputs[k], dtype=np.float32))
    x = gi("L0_attn_x")
    base = {
        "ln_g": gi("L0_attn_ln_g").reshape(8, 128),
        "ln_b": gi("L0_attn_ln_b").reshape(8, 128),
        "in_w": gi("L0_attn_in_w"),
        "conv_w": gi("L0_attn_conv_w").reshape(DI, 4),
        "conv_b": gi("L0_attn_conv_b").reshape(16, 128),
        "xproj_w": gi("L0_attn_xproj_w"),
        "dt_w": gi("L0_attn_dt_w").reshape(16, 128),
        "dt_b": gi("L0_attn_dt_b").reshape(16, 128),
        "A_log": gi("L0_attn_A_log").reshape(1, DS),
        "Dvec": gi("L0_attn_D").reshape(16, 128),
        "out_w": gi("L0_attn_out_w"),
        "wr": gi("L0_ffn_wr"),
        "br": gi("L0_ffn_br").reshape(NE, 1),
        "w1": gi("L0_ffn_w1"),
        "b1": gi("L0_ffn_b1").reshape(NE * 32, 128),
        "w2": gi("L0_ffn_w2"),
        "b2": gi("L0_ffn_b2").reshape(1, NE * D),
    }
    nc = build_nc()
    in_maps = [dict(base, x=np.ascontiguousarray(x[c])) for c in range(8)]
    res = run_bass_kernel_spmd(nc, in_maps, core_ids=list(range(8)))
    return np.stack([np.asarray(res.results[c]["out"]) for c in range(8)], axis=0)


if __name__ == "__main__":
    nc = build_nc()
    print("build ok")


# version 31
# speedup vs baseline: 1.1916x; 1.1916x over previous
import contextlib
import numpy as np

import concourse.bass as bass
from concourse.bass import IndirectOffsetOnAxis
import concourse.mybir as mybir
from concourse import bacc
from concourse.bass_utils import run_bass_kernel_spmd

F32 = mybir.dt.float32
BF16 = mybir.dt.bfloat16
I32 = mybir.dt.int32
I16 = mybir.dt.int16
U32 = mybir.dt.uint32
AF = mybir.ActivationFunctionType
ALU = mybir.AluOpType
AX = mybir.AxisListType

S = 4096
D = 1024
DI = 2048
DI2 = 4096
DS = 16
NE = 8
FF = 4096
EPS = 1e-5
NT = S // 128
TCH = 1024
NTC = S // TCH
NDB = DI // 128
NPAD = 768
NG = NPAD // 128
HPE = 4 * NG
WRC = NE * NPAD // 16


class Prog:

    def __init__(self):
        self.mode = "count"
        self.tags = {}
        self.counts = {}
        self.eng = None
        self.sems = {}
        self.vser = 0

    def start_count(self):
        self.mode = "count"
        self.counts = {}
        self.tags = {}
        self.vser = 0

    def start_emit(self, sems):
        self.mode = "emit"
        self.sems = sems
        self.vser = 0

    def vtag(self):
        self.vser += 1
        return ("_f", self.vser)

    def op(self, fn, inc=None):
        incs = [] if inc is None else (inc if isinstance(inc, list) else [inc])
        if self.mode == "count":
            for (sem, tag, n) in incs:
                c = self.counts.get(sem, 0) + n
                self.counts[sem] = c
                assert (sem, tag) not in self.tags, f"dup tag {sem} {tag}"
                self.tags[(sem, tag)] = c
        else:
            r = fn(self.eng)
            for (sem, tag, n) in incs:
                r.then_inc(self.sems[sem], n)

    def wait(self, sem, tag):
        if self.mode == "emit":
            v = self.tags.get((sem, tag))
            assert v is not None, f"missing tag {sem} {tag}"
            self.eng.wait_ge(self.sems[sem], v)

    def wait_maybe(self, sem, tag):
        if self.mode == "emit":
            v = self.tags.get((sem, tag))
            if v is not None:
                self.eng.wait_ge(self.sems[sem], v)

    def barrier(self, k):
        if self.mode == "emit":
            self.eng.nop().then_inc(self.sems["phb"], 1)
            self.eng.wait_ge(self.sems["phb"], 5 * k)


SEMS = ["phb", "vself", "aself", "gi",
        "sstg", "sprm", "g3",
        "sx0", "sx1", "sac0", "sac1", "gx0", "gx1", "v0", "a0", "t0",
        "gw0", "gw1", "gw2", "gw3", "sw0", "sw1", "gxs0", "gxs1", "gsz0", "gsz1", "t1", "v1", "a1",
        "sxst0", "sxst1", "t3", "v3",
        "gy0", "gy1", "v4", "a4",
        "scq0", "scq1", "sbq0", "sbq1", "sdr0", "sdr1", "ssg0", "ssg1",
        "sout0", "sout1",
        "gpm", "gidx", "ggx0", "gsc",
        "sy20", "sy21", "sy22", "sal0", "sal1", "t5", "v5",
        "g60", "g61", "t6", "v6", "a6"]


DBG = {}


def build_nc(phases=7):
    nc = bass.Bass("TRN2", target_bir_lowering=False)

    x_h = nc.dram_tensor("x", [S, D], F32, kind="ExternalInput").ap()
    lng_h = nc.dram_tensor("ln_g", [8, 128], F32, kind="ExternalInput").ap()
    lnb_h = nc.dram_tensor("ln_b", [8, 128], F32, kind="ExternalInput").ap()
    inw_h = nc.dram_tensor("in_w", [DI2, D], F32, kind="ExternalInput").ap()
    convw_h = nc.dram_tensor("conv_w", [DI, 4], F32, kind="ExternalInput").ap()
    convb_h = nc.dram_tensor("conv_b", [16, 128], F32, kind="ExternalInput").ap()
    xpw_h = nc.dram_tensor("xproj_w", [33, DI], F32, kind="ExternalInput").ap()
    dtw_h = nc.dram_tensor("dt_w", [16, 128], F32, kind="ExternalInput").ap()
    dtb_h = nc.dram_tensor("dt_b", [16, 128], F32, kind="ExternalInput").ap()
    alog_h = nc.dram_tensor("A_log", [1, DS], F32, kind="ExternalInput").ap()
    dvec_h = nc.dram_tensor("Dvec", [16, 128], F32, kind="ExternalInput").ap()
    outw_h = nc.dram_tensor("out_w", [D, DI], F32, kind="ExternalInput").ap()
    wr_h = nc.dram_tensor("wr", [D, NE], F32, kind="ExternalInput").ap()
    br_h = nc.dram_tensor("br", [NE, 1], F32, kind="ExternalInput").ap()
    w1_h = nc.dram_tensor("w1", [NE, D, FF], F32, kind="ExternalInput").ap()
    b1_h = nc.dram_tensor("b1", [NE * 32, 128], F32, kind="ExternalInput").ap()
    w2_h = nc.dram_tensor("w2", [NE, FF, D], F32, kind="ExternalInput").ap()
    b2_h = nc.dram_tensor("b2", [1, NE * D], F32, kind="ExternalInput").ap()
    out_h = nc.dram_tensor("out", [S, D], F32, kind="ExternalOutput").ap()

    xn_cm_h = nc.dram_tensor("xn_cm", [128, 8 * S], BF16, kind="Internal").ap()
    xs_hbm = nc.dram_tensor("xs_sc", [128, NDB * S], BF16, kind="Internal").ap()
    sz_hbm = nc.dram_tensor("sz_sc", [128, NDB * S], BF16, kind="Internal").ap()
    y2_hbm = nc.dram_tensor("y2_sc", [128, NDB * S], BF16, kind="Internal").ap()
    acc_hbm = nc.dram_tensor("acc_sc", [S + 1, D], BF16, kind="Internal").ap()
    drrow_h = nc.dram_tensor("dr_sc", [1, S], BF16, kind="Internal").ap()
    rowbc_h = nc.dram_tensor("bc_sc", [32, S], BF16, kind="Internal").ap()
    bbc_h = nc.dram_tensor("bbc_sc", [64, TCH], BF16, kind="Internal").ap()
    cbc_h = nc.dram_tensor("cbc_sc", [64, TCH], BF16, kind="Internal").ap()
    arow_h = nc.dram_tensor("a_sc", [1, DS], F32, kind="Internal").ap()
    dA_hbm = nc.dram_tensor("dA_sc", [128, NDB * DS * S], BF16, kind="Internal").ap()
    xn_tm_h = nc.dram_tensor("xn_tm", [S + 1, D], BF16, kind="Internal").ap()
    perm_h = nc.dram_tensor("perm_sc", [NE * NPAD, 1], I32, kind="Internal").ap()
    u_hbm = nc.dram_tensor("u_sc", [128, NDB * S], BF16, kind="Internal").ap()

    es = contextlib.ExitStack()
    with es:
        sbt = lambda name, shape, dt: es.enter_context(nc.sbuf_tensor(name, shape, dt))
        pst = lambda name, shape, dt: es.enter_context(nc.psum_tensor(name, shape, dt))

        identf = sbt("identf", [128, 128], F32)
        identb = sbt("identb", [128, 128], BF16)

        g_col = sbt("g_col", [128, 8], F32)
        bcol = sbt("bcol", [128, 8], F32)
        convw_col = sbt("convw_col", [128, 64], F32)
        convb_col = sbt("convb_col", [128, 16], F32)
        dtw_col = sbt("dtw_col", [128, 16], F32)
        dtb_col = sbt("dtb_col", [128, 16], F32)
        d_col = sbt("d_col", [128, 16], F32)
        a_col = sbt("a_col", [128, DS], F32)
        wr_sb = sbt("wr_sb", [128, 8 * NE], F32)
        br_sb = sbt("br_sb", [NE, 1], F32)
        b1_col = sbt("b1_col", [128, NE * 32], F32)
        b2cur = [sbt("b2c0", [1, D], BF16), sbt("b2c1", [1, D], BF16)]
        ones1 = sbt("ones1", [1, 128], BF16)
        epsrow = sbt("epsrow", [128, NE], F32)
        oh_tok = sbt("oh_tok", [128, NT * NE], F32)
        ones8c = sbt("ones8c", [128, NE], F32)
        erow = sbt("erow", [128, NE], F32)
        tokid = sbt("tokid", [128, NT], I32)
        carry = sbt("carry", [128, NDB * DS], F32)


        AR1 = sbt("AR1", [128, 15872], F32)
        AR2 = sbt("AR2", [128, 18432], F32)
        AR3 = sbt("AR3", [128, 16384], F32)
        small = sbt("small", [128, 128], F32)

        def cv(arena, off, n, dt=F32):
            v = arena[:, off:off + n]
            return v if dt is F32 else v.bitcast(dt)

        idx_i = cv(AR1, 0, 128).bitcast(I32)
        pstage = cv(AR1, 128, 128)

        red1 = small[:, 0:1]; red2 = small[:, 1:2]
        mean1 = small[:, 2:3]; var1 = small[:, 4:5]
        t_a = small[:, 6:7]; t_b = small[:, 7:8]
        mean2 = small[:, 8:9]; var2 = small[:, 10:11]
        rstd1 = small[:, 12:13]; rstd2 = small[:, 13:14]
        lntmp = small[:, 14:15]; lntmp2 = small[:, 15:16]
        mx8 = small[:, 16:17]
        ps8 = small[:, 80:88]
        e18 = small[:, 88:96]
        zcol = small[:, 96:97]
        bn6 = small[:, 98:110]
        bnag = small[:, 110:112]
        tmp8a = small[:, 98:106]
        tmp8b = small[:, 106:114]
        r1c = small[:, 114:115]
        r2c = small[:, 115:116]
        slc = small[:, 116:117]
        ovc = small[:, 117:118]
        sic = small[:, 118:119].bitcast(I32)
        qic = small[:, 119:120].bitcast(I32)
        cic = small[:, 120:121].bitcast(I32)
        adj8 = small[:, 24:32]
        sc8 = small[:, 32:40]
        astage = small[0:1, 40:56]
        aneg = small[0:1, 56:72]
        vsyncb = small[:, 72:73]
        cEPS = small[:, 73:74]
        cM05 = small[:, 74:75]
        cM1 = small[:, 75:76]
        cONE = small[:, 76:77]

        PSA = pst("PSA", [128, 1024], F32)
        PSB = pst("PSB", [128, 1024], F32)
        PSC = pst("PSC", [128, 1024], F32)
        PSD = pst("PSD", [128, 1024], F32)

        DBG.clear()
        DBG.update(dict(acc_hbm=acc_hbm, perm_h=perm_h, xn_tm_h=xn_tm_h, xn_cm_h=xn_cm_h, xs_hbm=xs_hbm, sz_hbm=sz_hbm, out_h=out_h, dA_hbm=dA_hbm, u_hbm=u_hbm,
                        y2_hbm=y2_hbm, drrow_h=drrow_h, rowbc_h=rowbc_h))
        p = Prog()

        PLIST = [("lng", 0, 8, g_col), ("lnb", 0, 8, bcol), ("convb", 0, 16, convb_col),
                 ("dtw", 0, 16, dtw_col), ("dtb", 0, 16, dtb_col), ("dvec", 0, 16, d_col)]

        def p0(which):
            if which == "sync":
                for kt in range(8):
                    p.op(lambda e, kt=kt: e.dma_start(out=wr_sb[:, kt * NE:(kt + 1) * NE],
                                                      in_=wr_h[kt * 128:(kt + 1) * 128, :]),
                         inc=("sprm", ("wr", kt), 16))
                p.op(lambda e: e.dma_start(out=br_sb[:, :], in_=br_h[:, :]), inc=("sprm", "br", 16))
                p.op(lambda e: e.dma_start(out=convw_col.rearrange("q (db j) -> q db j", j=4),
                                           in_=convw_h.rearrange("(db q) j -> q db j", q=128)),
                     inc=("sprm", "convw", 16))
                p.op(lambda e: e.dma_start(out=astage, in_=alog_h[:, :]), inc=("sprm", "alog", 16))
                for idx2, ((nm, r0, k), srcap) in enumerate(zip([(n, r, k) for n, r, k, _ in PLIST],
                                            [lng_h, lnb_h, convb_h, dtw_h, dtb_h, dvec_h])):
                    if idx2 > 0:
                        p.wait("t3", ("pT", PLIST[idx2 - 1][0]))
                    p.op(lambda e, k=k, srcap=srcap: e.dma_start(out=pstage[0:k, :], in_=srcap[:, :]),
                         inc=("sstg", ("p0", nm), 16))
                p.wait("t3", ("pT", PLIST[-1][0]))
                for ee in range(NE):
                    if ee > 0:
                        p.wait("t3", ("b1T", ee - 1))
                    p.op(lambda e, ee=ee: e.dma_start(out=pstage[32:64, :],
                                                      in_=b1_h[ee * 32:(ee + 1) * 32, :]),
                         inc=("sstg", ("b1s", ee), 16))
                p.wait("sprm", "alog")
                p.wait("sstg", ("b1s", NE - 1))
            elif which == "gpsimd":
                p.op(lambda e: e.iota(idx_i[:, :], [[1, 128]], base=0, channel_multiplier=-1),
                     inc=("gi", "iota", 1))
                p.op(lambda e: e.iota(erow[:, :], [[1, NE]], base=0, channel_multiplier=0,
                                      allow_small_or_imprecise_dtypes=True),
                     inc=("gi", "iotae", 1))
                p.op(lambda e: e.iota(tokid[:, :], [[128, NT]], base=0, channel_multiplier=1),
                     inc=("gi", "iotat", 1))

                p.wait("vself", ("p0", "aneg"))
                p.op(lambda e: e.dma_start(out=arow_h[:, :], in_=aneg), inc=("g3", "arowst", 16))
                p.wait("g3", "arowst")
                p.op(lambda e: e.dma_start(out=a_col[:, :],
                                           in_=arow_h[:, :].partition_broadcast(128).squeeze(1)),
                     inc=("g3", "acol", 16))
                p.wait("g3", "acol")
            elif which == "scalar":
                p.wait("sprm", "alog")
                p.op(lambda e: e.activation(aneg, astage, AF.Exp), inc=("aself", ("p0", "expa"), 1))
            elif which == "vector":
                p.wait("gi", "iota")
                p.op(lambda e: e.tensor_scalar(identf[:, :], idx_i[:, :], 0, None, op0=ALU.is_equal),
                     inc=("vself", ("p0", "idf"), 1))
                p.op(lambda e: e.tensor_scalar(identb[:, :], idx_i[:, :], 0, None, op0=ALU.is_equal),
                     inc=("vself", ("p0", "idb"), 1))
                p.op(lambda e: e.tensor_scalar(epsrow[:, :], idx_i[:, 0:NE], 1e-30, None, op0=ALU.mult),
                     inc=("vself", ("p0", "epsr"), 1))
                p.op(lambda e: e.memset(carry[:, :], 0.0))
                p.op(lambda e: e.memset(ones1[:, :], 1.0))
                p.op(lambda e: e.memset(ones8c[:, :], 1.0))
                p.op(lambda e: e.memset(zcol, 0.0))
                p.wait("gi", "iotae")
                p.op(lambda e: e.tensor_scalar(erow[:, :], erow[:, :], float(NPAD), None, op0=ALU.mult),
                     inc=("vself", ("p0", "erow"), 1))
                p.op(lambda e: e.memset(cEPS, EPS))
                p.op(lambda e: e.memset(cM05, -0.5))
                p.op(lambda e: e.memset(cM1, -1.0))
                p.op(lambda e: e.memset(cONE, 1.0))
                for nm, _, k, dst in PLIST:
                    p.wait("t3", ("pT", nm))
                    p.op(lambda e, dst=dst, k=k: e.tensor_copy(dst[:, 0:k], PSC[:, 0:k]),
                         inc=("vself", ("pev", nm), 1))
                for ee in range(NE):
                    p.wait("t3", ("b1T", ee))
                    p.op(lambda e, ee=ee: e.tensor_copy(b1_col[:, ee * 32:(ee + 1) * 32], PSA[:, 0:32]),
                         inc=("vself", ("evb1", ee), 1))
                p.wait("aself", ("p0", "expa"))
                p.op(lambda e: e.tensor_scalar(aneg, aneg, -1.0, None, op0=ALU.mult),
                     inc=("vself", ("p0", "aneg"), 1))
            elif which == "tensor":
                p.wait("vself", ("p0", "idf"))
                for i, (nm, r0, k, _) in enumerate(PLIST):
                    p.wait("sstg", ("p0", nm))
                    if i > 0:
                        p.wait("vself", ("pev", PLIST[i - 1][0]))
                    p.op(lambda e, k=k: e.transpose(PSC[:, 0:k], pstage[0:k, :], identf[0:k, 0:k]),
                         inc=("t3", ("pT", nm), 1))
                for ee in range(NE):
                    p.wait("sstg", ("b1s", ee))
                    if ee > 0:
                        p.wait("vself", ("evb1", ee - 1))
                    p.op(lambda e: e.transpose(PSA[:, 0:32], pstage[32:64, :], identf[32:64, 32:64]),
                         inc=("t3", ("b1T", ee), 1))

        xn2_sb = cv(AR2, 0, 16384, BF16)
        xbuf = [cv(AR1, 0, 1024), cv(AR1, 1024, 1024)]
        sqb = cv(AR1, 2048, 1024)
        xnbuf = [cv(AR1, 3072, 1024), cv(AR1, 4096, 1024)]
        xnn2b = [cv(AR1, 5120, 1024), cv(AR1, 6144, 1024)]
        accb = [cv(AR1, 7168, 512, BF16), cv(AR1, 7680, 512, BF16)]
        xnT = [cv(AR1, 9216, 1024), cv(AR1, 10240, 1024)]
        sc_eb = cv(AR1, 11264, 128)
        oh_et = cv(AR3, 0, 4096)
        pos_et = cv(AR3, 4096, 4096)
        slotub = cv(AR3, 8224, 32).bitcast(U32)
        slotib = cv(AR3, 8224, 32).bitcast(I32)
        zer8 = cv(AR3, 8256, 1)
        ones_et = cv(AR3, 8448, 4096)
        sent_f = cv(AR3, 12544, 128)
        sent_i = cv(AR3, 12672, 128).bitcast(I32)
        zrow16 = cv(AR3, 12800, 512, BF16)
        ps_t = [PSA, PSB]
        ps_t2 = PSC
        ps_sc = PSD[:, 0:128]
        ps_scT = PSD[:, 512:520]
        ps_ohT = PSD[:, 640:768]
        ps_psT = PSD[:, 528:536]

        def p1v_A(i):
            par = i % 2
            p.wait("sx0" if par == 0 else "sx1", ("x", i))
            if i >= 2:
                p.wait("sac0" if par == 0 else "sac1", ("st", i - 2))
            t1, t2, t3_, t4, t5, t6 = [p.vtag() for _ in range(6)]
            p.op(lambda e, par=par: e.tensor_tensor(sqb[:, :], xbuf[par][:, :], xbuf[par][:, :],
                                                    op=ALU.mult), inc=("vself", t1, 1))
            p.op(lambda e, par=par: e.reduce_sum(red1, xbuf[par][:, :], axis=AX.X),
                 inc=("vself", t2, 1))
            p.wait("vself", t1)
            p.op(lambda e: e.reduce_sum(red2, sqb[:, :], axis=AX.X), inc=("vself", t3_, 1))
            p.wait("vself", t2)
            p.op(lambda e: e.tensor_scalar(mean1, red1, 1.0 / D, None, op0=ALU.mult),
                 inc=("vself", t4, 1))
            p.wait("vself", t3_)
            p.op(lambda e: e.tensor_scalar(t_a, red2, 1.0 / D, None, op0=ALU.mult),
                 inc=("vself", t5, 1))
            p.wait("vself", t4)
            p.op(lambda e: e.tensor_tensor(t_b, mean1, mean1, op=ALU.mult), inc=("vself", t6, 1))
            p.wait("vself", t5)
            p.wait("vself", t6)
            p.op(lambda e: e.tensor_tensor(var1, t_a, t_b, op=ALU.subtract),
                 inc=("v0", ("var1", i), 1))

        def p1v_B(i):
            par = i % 2
            p.wait("a0", ("rstd1", i))
            if i >= 2:
                p.wait("gx0" if par == 0 else "gx1", ("sp", i - 2))
            p.op(lambda e, par=par: e.tensor_scalar(xnbuf[par][:, :], xbuf[par][:, :], mean1, rstd1,
                                                    op0=ALU.subtract, op1=ALU.mult),
                 inc=("v0", ("xn", i), 1))
            p.wait("v0", ("xn", i))
            p.op(lambda e, par=par: e.tensor_tensor(accb[par][:, :], xbuf[par][:, :], xnbuf[par][:, :],
                                                    op=ALU.add), inc=("v0", ("acc", i), 1))
            u1, u2, u3, u4, u5, u6 = [p.vtag() for _ in range(6)]
            p.op(lambda e, par=par: e.tensor_tensor(sqb[:, :], xnbuf[par][:, :], xnbuf[par][:, :],
                                                    op=ALU.mult), inc=("vself", u1, 1))
            p.op(lambda e, par=par: e.reduce_sum(red1, xnbuf[par][:, :], axis=AX.X),
                 inc=("vself", u2, 1))
            p.wait("vself", u1)
            p.op(lambda e: e.reduce_sum(red2, sqb[:, :], axis=AX.X), inc=("vself", u3, 1))
            p.wait("vself", u2)
            p.op(lambda e: e.tensor_scalar(mean2, red1, 1.0 / D, None, op0=ALU.mult),
                 inc=("vself", u4, 1))
            p.wait("vself", u3)
            p.op(lambda e: e.tensor_scalar(t_a, red2, 1.0 / D, None, op0=ALU.mult),
                 inc=("vself", u5, 1))
            p.wait("vself", u4)
            p.op(lambda e: e.tensor_tensor(t_b, mean2, mean2, op=ALU.mult), inc=("vself", u6, 1))
            p.wait("vself", u5)
            p.wait("vself", u6)
            p.op(lambda e: e.tensor_tensor(var2, t_a, t_b, op=ALU.subtract),
                 inc=("v0", ("var2", i), 1))

        def p1v_C(i):
            par = i % 2
            p.wait("a0", ("rstd2", i))
            p.op(lambda e, par=par: e.tensor_scalar(xnn2b[par][:, :], xnbuf[par][:, :], mean2, rstd2,
                                                    op0=ALU.subtract, op1=ALU.mult),
                 inc=("v0", ("xnn2", i), 1))

        def p1v_D(i):
            p.wait("t0", ("scT", i))
            w1t, w2t, w3t = p.vtag(), p.vtag(), p.vtag()
            p.op(lambda e: e.tensor_copy(sc8, ps_scT[:, :]), inc=("vself", w1t, 1))
            p.wait("vself", w1t)
            p.op(lambda e: e.tensor_tensor(adj8, sc8, epsrow[:, :], op=ALU.subtract),
                 inc=("vself", w2t, 1))
            p.wait("vself", w2t)
            p.op(lambda e: e.reduce_max(mx8, adj8, axis=AX.X), inc=("vself", w3t, 1))
            p.wait("vself", w3t)
            w4t, w5t, w6t = p.vtag(), p.vtag(), p.vtag()
            p.op(lambda e, i=i: e.tensor_scalar(oh_tok[:, i * NE:(i + 1) * NE], adj8, mx8, None,
                                                op0=ALU.is_equal), inc=("vself", w4t, 1))
            p.wait("vself", w4t)
            p.op(lambda e, i=i: e.tensor_tensor_scan(ps8, ones8c[:, :],
                                                     oh_tok[:, i * NE:(i + 1) * NE],
                                                     zcol, op0=ALU.mult, op1=ALU.add),
                 inc=("vself", w5t, 1))
            p.wait("vself", w5t)
            p.op(lambda e: e.tensor_scalar(e18, ps8, 1.0, None, op0=ALU.is_equal),
                 inc=("vself", w6t, 1))
            p.wait("vself", w6t)
            p.op(lambda e, i=i: e.tensor_tensor(oh_tok[:, i * NE:(i + 1) * NE],
                                                oh_tok[:, i * NE:(i + 1) * NE], e18,
                                                op=ALU.mult), inc=("v0", ("oh", i), 1))

        def p1a_rstd1(i):
            p.wait("v0", ("var1", i))
            at = ("af1", i)
            p.op(lambda e: e.activation(lntmp, var1, AF.Ln, bias=cEPS), inc=("aself", at, 1))
            p.wait("aself", at)
            p.op(lambda e: e.activation(rstd1, lntmp, AF.Exp, scale=cM05),
                 inc=("a0", ("rstd1", i), 1))

        def p1a_rstd2(i):
            p.wait("v0", ("var2", i))
            at2 = ("af2", i)
            p.op(lambda e: e.activation(lntmp2, var2, AF.Ln, bias=cEPS), inc=("aself", at2, 1))
            p.wait("aself", at2)
            p.op(lambda e: e.activation(rstd2, lntmp2, AF.Exp, scale=cM05),
                 inc=("a0", ("rstd2", i), 1))

        def p1a_evacs(i):
            par = i % 2
            p.wait("t0", ("xnT", i))
            p.op(lambda e, par=par: e.activation(xnT[par][:, :], ps_t[par][:, :], AF.Copy),
                 inc=("a0", ("evT", i), 1))
            p.wait("t0", ("xnn2T", i))
            for j in range(8):
                inc = ("a0", ("aff", i), 1) if j == 7 else None
                p.op(lambda e, i=i, j=j: e.activation(
                    xn2_sb[:, j * S + i * 128:j * S + (i + 1) * 128],
                    ps_t2[:, j * 128:(j + 1) * 128], AF.Identity,
                    scale=g_col[:, j:j + 1], bias=bcol[:, j:j + 1]), inc=inc)
            p.wait("t0", ("mmsc", i))
            p.op(lambda e: e.activation(sc_eb[0:NE, :], ps_sc[0:NE, :], AF.Identity,
                                        bias=br_sb[:, 0:1]), inc=("a0", ("scev", i), 1))
            p.wait("t0", ("ohT", i))
            p.op(lambda e, i=i: e.activation(oh_et[0:NE, i * 128:(i + 1) * 128],
                                             ps_ohT[0:NE, :], AF.Copy),
                 inc=("a0", ("ohet", i), 1))

        def p1(which):
            if which == "scalar":
                p1a_rstd1(0)
                for i in range(NT):
                    p1a_rstd2(i)
                    if i + 1 < NT:
                        p1a_rstd1(i + 1)
                    p1a_evacs(i)
            if which == "vector":
                p1v_A(0)
                for i in range(NT):
                    p1v_B(i)
                    if i >= 1:
                        p1v_D(i - 1)
                    if i + 1 < NT:
                        p1v_A(i + 1)
                    p1v_C(i)
                p1v_D(NT - 1)
            for i in range(NT):
                par = i % 2
                if which == "sync":
                    if i >= 2:
                        p.wait("v0", ("acc", i - 2))
                    p.op(lambda e, i=i, par=par: e.dma_start(out=xbuf[par][:, :],
                                                             in_=x_h[i * 128:(i + 1) * 128, :]),
                         inc=("sx0" if par == 0 else "sx1", ("x", i), 16))
                    p.wait("v0", ("acc", i))
                    p.op(lambda e, i=i, par=par: e.dma_start(out=acc_hbm[i * 128:(i + 1) * 128, :],
                                                             in_=accb[par][:, :]),
                         inc=("sac0" if par == 0 else "sac1", ("st", i), 16))
                    if i == NT - 1:
                        p.wait("sx0", ("x", NT - 2))
                        p.wait("sx1", ("x", NT - 1))
                        p.wait("sac0", ("st", NT - 2))
                        p.wait("sac1", ("st", NT - 1))
                elif which == "gpsimd":
                    p.wait("v0", ("xn", i))
                    p.op(lambda e, i=i, par=par: e.dma_start(
                        out=xn_tm_h[i * 128:(i + 1) * 128, :],
                        in_=xnbuf[par][:, :]),
                        inc=("gx0" if par == 0 else "gx1", ("sp", i), 16))
                    if i == NT - 1:
                        p.wait("gx0", ("sp", NT - 2))
                        p.wait("gx1", ("sp", NT - 1))
                elif which == "vector":
                    pass
                elif which == "scalar":
                    pass
                elif which == "tensor":
                    p.wait("v0", ("xn", i))
                    if i >= 2:
                        p.wait("a0", ("evT", i - 2))
                    for j in range(8):
                        inc = ("t0", ("xnT", i), 1) if j == 7 else None
                        p.op(lambda e, par=par, j=j: e.transpose(ps_t[par][:, j * 128:(j + 1) * 128],
                                                                 xnbuf[par][:, j * 128:(j + 1) * 128],
                                                                 identf[:, :]), inc=inc)
                    p.wait("v0", ("xnn2", i))
                    if i >= 1:
                        p.wait("a0", ("aff", i - 1))
                    for j in range(8):
                        inc = ("t0", ("xnn2T", i), 1) if j == 7 else None
                        p.op(lambda e, par=par, j=j: e.transpose(ps_t2[:, j * 128:(j + 1) * 128],
                                                                 xnn2b[par][:, j * 128:(j + 1) * 128],
                                                                 identf[:, :]), inc=inc)
                    p.wait("a0", ("evT", i))
                    if i >= 1:
                        p.wait("v0", ("oh", i - 1))
                    for j in range(8):
                        inc = ("t0", ("mmsc", i), 1) if j == 7 else None
                        p.op(lambda e, par=par, j=j: e.matmul(ps_sc[0:NE, :], wr_sb[:, j * NE:(j + 1) * NE],
                                                              xnT[par][:, j * 128:(j + 1) * 128],
                                                              start=(j == 0), stop=(j == 7)), inc=inc)
                    p.wait("a0", ("scev", i))
                    if i >= 1:
                        p.wait("v0", ("oh", i - 1))
                    p.op(lambda e: e.transpose(ps_scT[:, :], sc_eb[0:NE, :], identf[0:NE, 0:NE]),
                         inc=("t0", ("scT", i), 1))
                    p.wait("v0", ("oh", i))
                    if i >= 1:
                        p.wait("a0", ("ohet", i - 1))
                    p.op(lambda e, i=i: e.transpose(ps_ohT[0:NE, :], oh_tok[:, i * NE:(i + 1) * NE],
                                                    identf[:, :]), inc=("t0", ("ohT", i), 1))

            if which == "vector":
                vt1, vt2 = p.vtag(), p.vtag()
                p.op(lambda e: e.memset(ones_et[0:8, :], 1.0))
                p.op(lambda e: e.memset(zer8[0:8, 0:1], 0.0), inc=("vself", vt1, 1))
                p.op(lambda e: e.memset(sent_f[:, 0:NE * NPAD // 128], float(S)), inc=("vself", vt2, 1))
                p.wait("vself", vt2)
                p.op(lambda e: e.tensor_scalar(sent_i[:, 0:NE * NPAD // 128],
                                               sent_f[:, 0:NE * NPAD // 128], 0.0, None,
                                               op0=ALU.add), inc=("v0", ("sent",), 1))
                p.op(lambda e: e.memset(zrow16[0:1, :], 0.0), inc=("v0", ("zrow",), 1))
                p.wait("vself", vt1)
                p.wait("a0", ("ohet", NT - 1))
                p.op(lambda e: e.tensor_tensor_scan(pos_et[0:8, :], ones_et[0:8, :], oh_et[0:8, :],
                                                    zer8[0:8, 0:1], op0=ALU.mult, op1=ALU.add),
                     inc=("v0", ("pos",), 1))
                for i in range(NT):
                    w = [p.vtag() for _ in range(8)]
                    p.wait("t0", ("psT", i))
                    p.op(lambda e, i=i: e.tensor_tensor(tmp8a, ps_psT,
                                                        oh_tok[:, i * NE:(i + 1) * NE], op=ALU.mult),
                         inc=("v0", ("slr", i), 1))
                    p.op(lambda e, i=i: e.tensor_tensor(tmp8b, erow[:, :],
                                                        oh_tok[:, i * NE:(i + 1) * NE], op=ALU.mult),
                         inc=("vself", w[0], 1))
                    p.wait("v0", ("slr", i))
                    p.op(lambda e: e.reduce_sum(r1c, tmp8a, axis=AX.X), inc=("vself", w[1], 1))
                    p.wait("vself", w[0])
                    p.op(lambda e: e.reduce_sum(r2c, tmp8b, axis=AX.X), inc=("vself", w[2], 1))
                    p.wait("vself", w[1])
                    p.wait("vself", w[2])
                    p.op(lambda e: e.tensor_tensor(slc, r1c, r2c, op=ALU.add), inc=("vself", w[3], 1))
                    p.op(lambda e: e.tensor_scalar(ovc, r1c, float(NPAD), 0.0,
                                                   op0=ALU.subtract, op1=ALU.max),
                         inc=("vself", w[4], 1))
                    p.wait("vself", w[4])
                    p.op(lambda e: e.tensor_scalar(ovc, ovc, 1.0e6, None, op0=ALU.mult),
                         inc=("vself", w[5], 1))
                    p.wait("vself", w[3])
                    p.wait("vself", w[5])
                    p.op(lambda e: e.tensor_tensor(slc, slc, ovc, op=ALU.add), inc=("vself", w[6], 1))
                    p.wait("vself", w[6])
                    p.op(lambda e: e.tensor_scalar(slc, slc, 1.0, None, op0=ALU.subtract),
                         inc=("vself", w[7], 1))
                    w8, w9, w10, w11, w12 = [p.vtag() for _ in range(5)]
                    p.wait("vself", w[7])
                    p.op(lambda e: e.tensor_scalar(slc, slc, 131072.0, None, op0=ALU.min),
                         inc=("vself", w8, 1))
                    p.wait("vself", w8)
                    p.op(lambda e: e.tensor_scalar(sic, slc, 0.0, None, op0=ALU.add),
                         inc=("vself", w9, 1))
                    p.wait("vself", w9)
                    p.op(lambda e: e.tensor_scalar(qic, sic, 15, None, op0=ALU.bitwise_and),
                         inc=("vself", w10, 1))
                    p.op(lambda e: e.tensor_scalar(cic, sic, 4, None, op0=ALU.logical_shift_right),
                         inc=("vself", w11, 1))
                    p.wait("vself", w10)
                    p.op(lambda e: e.tensor_scalar(qic, qic, WRC, None, op0=ALU.mult),
                         inc=("vself", w12, 1))
                    p.wait("vself", w11)
                    p.wait("vself", w12)
                    p.op(lambda e, i=i: e.tensor_tensor(slotib[:, i:i + 1], qic, cic, op=ALU.add),
                         inc=("v0", ("slot", i), 1))
            elif which == "tensor":
                p.wait("v0", ("pos",))
                for i in range(NT):
                    if i >= 1:
                        p.wait("v0", ("slr", i - 1))
                    p.op(lambda e, i=i: e.transpose(ps_psT, pos_et[0:8, i * 128:(i + 1) * 128],
                                                    identf[0:8, 0:8]), inc=("t0", ("psT", i), 1))
            elif which == "gpsimd":
                p.wait("v0", ("zrow",))
                p.op(lambda e: e.dma_start(out=xn_tm_h[S:S + 1, :], in_=zrow16[0:1, :]),
                     inc=("gpm", ("zr",), 16))
                p.wait("v0", ("sent",))
                p.op(lambda e: e.dma_start(
                    out=perm_h.rearrange("(a b) one -> a (b one)", b=NE * NPAD // 128),
                    in_=sent_i[:, 0:NE * NPAD // 128]), inc=("gpm", ("pref",), 16))
                p.wait("gpm", ("pref",))
                for i in range(NT):
                    p.wait("v0", ("slot", i))
                    p.op(lambda e, i=i: e.indirect_dma_start(
                        out=perm_h, out_offset=IndirectOffsetOnAxis(slotub[:, i:i + 1], 0),
                        in_=tokid[:, i:i + 1], in_offset=None,
                        bounds_check=NE * NPAD - 1, oob_is_err=False),
                        inc=("gpm", ("psc", i), 16))
                p.wait("gpm", ("psc", NT - 1))

        wbf = [cv(AR1, 0, 512, BF16), cv(AR1, 512, 512, BF16),
               cv(AR1, 14352, 512, BF16), cv(AR1, 14864, 512, BF16)]
        wTs = [cv(AR1, 1024, 512, BF16), cv(AR1, 1536, 512, BF16)]
        xi_db = [cv(AR1, 2048, 4104), cv(AR1, 6152, 4104)]
        xc_a = cv(AR1, 10256, 4096)
        xc_b = cv(AR3, 0, 4096)
        xsz = [cv(AR3, 4096, 2048, BF16), cv(AR3, 6144, 2048, BF16)]
        zsz = [cv(AR3, 8192, 2048, BF16), cv(AR3, 10240, 2048, BF16)]
        wstg = [cv(AR3, 12288, 1024), cv(AR3, 13312, 1024)]
        ps_w = [PSA[:, 0:512].bitcast(BF16), PSA[:, 512:1024].bitcast(BF16)]
        ps_xz = [PSB[:, 0:512], PSB[:, 512:1024]]

        def p2(which):
            for mi in range(32):
                par = mi % 2
                is_xi = mi < 16
                if which == "sync":
                    if mi >= 2:
                        p.wait("v1", ("wcp", mi - 2))
                    p.op(lambda e, mi=mi: e.dma_start(out=wstg[mi % 2][:, :],
                                                      in_=inw_h[mi * 128:(mi + 1) * 128, :]),
                         inc=("sw0" if mi % 2 == 0 else "sw1", ("wst", mi), 16))
                    if mi == 31:
                        p.wait("sw0", ("wst", 30))
                        p.wait("sw1", ("wst", 31))
                if which == "gpsimd":
                    if is_xi:
                        p.wait("a1", ("xs", mi))
                        p.op(lambda e, mi=mi, par=par: e.dma_start(
                            out=xs_hbm[:, mi * S:(mi + 1) * S], in_=xsz[par][:, :]),
                            inc=("gxs0" if par == 0 else "gxs1", ("xsst", mi), 16))
                    else:
                        p.wait("a1", ("zev", mi, 7))
                        p.op(lambda e, mi=mi: e.dma_start(
                            out=sz_hbm[:, (mi - 16) * S:(mi - 15) * S], in_=zsz[(mi - 16) % 2][:, :]),
                            inc=("gsz0" if (mi - 16) % 2 == 0 else "gsz1", ("szst", mi), 16))
                    if mi == 31:
                        p.wait("gxs0", ("xsst", 14))
                        p.wait("gxs1", ("xsst", 15))
                        p.wait("gsz0", ("szst", 30))
                        p.wait("gsz1", ("szst", 31))
                elif which == "tensor":
                    p.wait("v1", ("wcp", mi))
                    if mi >= 2:
                        p.wait("v1", ("wev", mi - 2))
                    for j in range(8):
                        inc = ("t1", ("wT", mi), 1) if j == 7 else None
                        p.op(lambda e, par=par, j=j: e.transpose(
                            ps_w[par][:, j * 128:(j + 1) * 128], wbf[mi % 4][:, j * 128:(j + 1) * 128],
                            identb[:, :]), inc=inc)
                    p.wait("v1", ("wev", mi))
                    for c in range(8):
                        q = mi * 8 + c
                        if q >= 2:
                            mi2, c2 = divmod(q - 2, 8)
                            if mi2 < 16:
                                p.wait("v1", ("xzev", mi2, c2))
                            else:
                                p.wait("a1", ("zev", mi2, c2))
                        for j in range(8):
                            inc = ("t1", ("xz", mi, c), 1) if j == 7 else None
                            p.op(lambda e, par=par, j=j, c=c, q=q: e.matmul(
                                ps_xz[q % 2][:, :], wTs[par][:, j * 128:(j + 1) * 128],
                                xn2_sb[:, j * S + c * 512:j * S + (c + 1) * 512],
                                start=(j == 0), stop=(j == 7)), inc=inc)
                elif which == "vector":
                    def wcp(m):
                        p.wait("sw0" if m % 2 == 0 else "sw1", ("wst", m))
                        if m >= 4:
                            p.wait("t1", ("wT", m - 4))
                        p.op(lambda e, m=m: e.tensor_copy(wbf[m % 4][:, :], wstg[m % 2][:, :]),
                             inc=("v1", ("wcp", m), 1))
                    if mi == 0:
                        wcp(0)
                        wcp(1)
                    if mi + 2 < 32:
                        wcp(mi + 2)
                    p.wait("t1", ("wT", mi))
                    p.op(lambda e, par=par: e.tensor_copy(wTs[par][:, :], ps_w[par][:, :]),
                         inc=("v1", ("wev", mi), 1))
                    if is_xi:
                        p.op(lambda e, par=par: e.memset(xi_db[par][:, 0:3], 0.0))
                        for c in range(8):
                            q = mi * 8 + c
                            p.wait("t1", ("xz", mi, c))
                            p.op(lambda e, par=par, c=c, q=q: e.tensor_copy(
                                xi_db[par][:, 3 + c * 512:3 + (c + 1) * 512], ps_xz[q % 2][:, :]),
                                inc=("v1", ("xzev", mi, c), 1))
                        p.wait("v1", ("xzev", mi, 7))
                        c0t, c1t, c2t = p.vtag(), p.vtag(), p.vtag()
                        p.op(lambda e, par=par, mi=mi: e.tensor_scalar(
                            xc_a[:, :], xi_db[par][:, 0:S], convw_col[:, mi * 4:mi * 4 + 1], None,
                            op0=ALU.mult), inc=("vself", c0t, 1))
                        p.wait("vself", c0t)
                        if mi >= 1:
                            p.wait("a1", ("xs", mi - 1))
                        p.op(lambda e, par=par, mi=mi: e.scalar_tensor_tensor(
                            xc_b[:, :], xi_db[par][:, 1:1 + S], convw_col[:, mi * 4 + 1:mi * 4 + 2],
                            xc_a[:, :], op0=ALU.mult, op1=ALU.add), inc=("vself", c1t, 1))
                        p.wait("vself", c1t)
                        p.op(lambda e, par=par, mi=mi: e.scalar_tensor_tensor(
                            xc_a[:, :], xi_db[par][:, 2:2 + S], convw_col[:, mi * 4 + 2:mi * 4 + 3],
                            xc_b[:, :], op0=ALU.mult, op1=ALU.add), inc=("vself", c2t, 1))
                        p.wait("vself", c2t)
                        p.op(lambda e, par=par, mi=mi: e.scalar_tensor_tensor(
                            xc_b[:, :], xi_db[par][:, 3:3 + S], convw_col[:, mi * 4 + 3:mi * 4 + 4],
                            xc_a[:, :], op0=ALU.mult, op1=ALU.add), inc=("v1", ("conv", mi), 1))
                elif which == "scalar":
                    if is_xi:
                        p.wait("v1", ("conv", mi))
                        if mi >= 2:
                            p.wait("gxs0" if par == 0 else "gxs1", ("xsst", mi - 2))
                        p.op(lambda e, par=par, mi=mi: e.activation(
                            xsz[par][:, :], xc_b[:, :], AF.Silu, bias=convb_col[:, mi:mi + 1]),
                            inc=("a1", ("xs", mi), 1))
                    else:
                        zp = (mi - 16) % 2
                        if mi >= 18:
                            p.wait("gsz0" if zp == 0 else "gsz1", ("szst", mi - 2))
                        for c in range(8):
                            p.wait("t1", ("xz", mi, c))
                            p.op(lambda e, zp=zp, c=c, q=mi * 8 + c: e.activation(
                                zsz[zp][:, c * 512:(c + 1) * 512], ps_xz[q % 2][:, :], AF.Silu),
                                inc=("a1", ("zev", mi, c), 1))

        xpwst = cv(AR1, 0, 2048)
        xp_sb = cv(AR1, 2048, 4096)
        xpw_bf = cv(AR1, 6144, 1024, BF16)
        xprojT = cv(AR1, 7168, 272, BF16)
        xst = [cv(AR3, 0, 4096, BF16), cv(AR3, 4096, 4096, BF16)]
        ps_xpT = PSA[:, 0:272].bitcast(BF16)
        ps_xp = PSB[:, 0:512]

        def p3(which):
            if which == "sync":
                p.op(lambda e: e.dma_start(out=xpwst[0:33, :], in_=xpw_h[:, :]), inc=("sstg", "xpw", 16))
                for c in range(8):
                    if c >= 2:
                        p.wait("v3", ("xpev", c - 2))
                    for kt in range(16):
                        sm = "sxst0" if c % 2 == 0 else "sxst1"
                        p.op(lambda e, c=c, kt=kt: e.dma_start(
                            out=xst[c % 2][:, kt * 512:(kt + 1) * 512],
                            in_=xs_hbm[:, kt * S + c * 512:kt * S + (c + 1) * 512]),
                            inc=(sm, ("xst", c, kt), 16))
                p.wait("sxst0", ("xst", 6, 15))
                p.wait("sxst1", ("xst", 7, 15))
            elif which == "vector":
                p.wait("sstg", "xpw")
                p.op(lambda e: e.tensor_copy(xpw_bf[0:33, :], xpwst[0:33, :]), inc=("v3", "cast", 1))
                p.wait("t3", "xpT")
                p.op(lambda e: e.tensor_copy(xprojT[:, :], ps_xpT[:, 0:544]), inc=("v3", "Tev", 1))
                for c in range(8):
                    p.wait("t3", ("xp", c))
                    p.op(lambda e, c=c: e.tensor_copy(xp_sb[0:33, c * 512:(c + 1) * 512], ps_xp[0:33, :]),
                         inc=("v3", ("xpev", c), 1))
            elif which == "tensor":
                p.wait("v3", "cast")
                for kt in range(16):
                    inc = ("t3", "xpT", 1) if kt == 15 else None
                    p.op(lambda e, kt=kt: e.transpose(ps_xpT[:, kt * 34:kt * 34 + 33],
                                                      xpw_bf[0:33, kt * 128:(kt + 1) * 128], identb[0:33, 0:33]),
                         inc=inc)
                p.wait("v3", "Tev")
                for c in range(8):
                    if c >= 1:
                        p.wait("v3", ("xpev", c - 1))
                    p.wait("sxst0" if c % 2 == 0 else "sxst1", ("xst", c, 15))
                    for kt in range(16):
                        inc = ("t3", ("xp", c), 1) if kt == 15 else None
                        p.op(lambda e, c=c, kt=kt: e.matmul(ps_xp[0:33, :],
                                                            xprojT[:, kt * 34:kt * 34 + 33],
                                                            xst[c % 2][:, kt * 512:(kt + 1) * 512],
                                                            start=(kt == 0), stop=(kt == 15)), inc=inc)
            elif which == "gpsimd":
                p.wait("v3", ("xpev", 7))
                p.op(lambda e: e.dma_start(out=drrow_h[:, :], in_=xp_sb[0:1, :]), inc=("g3", "dr", 16))
                for tc in range(NTC):
                    p.op(lambda e, tc=tc: e.dma_start(
                        out=bbc_h[tc * 16:(tc + 1) * 16, :],
                        in_=xp_sb[1:17, tc * TCH:(tc + 1) * TCH]), inc=("g3", ("bbc", tc), 16))
                    p.op(lambda e, tc=tc: e.dma_start(
                        out=cbc_h[tc * 16:(tc + 1) * 16, :],
                        in_=xp_sb[17:33, tc * TCH:(tc + 1) * TCH]), inc=("g3", ("cbc", tc), 16))
                p.wait("g3", ("cbc", NTC - 1))

        Bq = [cv(AR1, 0, 2048, BF16), cv(AR1, 2048, 2048, BF16)]
        Cq = [cv(AR1, 4096, 2048, BF16), cv(AR1, 6144, 2048, BF16)]
        dA3 = [cv(AR1, 8192 + 512 * r, 512, BF16) for r in range(5)]
        idx16 = cv(AR1, 10752, WRC // 2).bitcast(I16)
        x02 = [cv(AR1, 11264, 512, BF16), cv(AR1, 11776, 512, BF16)]
        ab2 = [cv(AR1, 12288, 512, BF16), cv(AR1, 12800, 512, BF16)]
        delta2 = [cv(AR1, 13312, 512, BF16), cv(AR1, 13824, 512, BF16)]
        drq = [cv(AR1, 14336, 512, BF16), cv(AR1, 14848, 512, BF16)]
        e_sb = cv(AR1, 15360, 512, BF16)
        oacc16 = cv(AR2, 0, NG * 1024, BF16)
        w1s2 = [cv(AR2, 6144, 4096, BF16), cv(AR2, 10240, 4096, BF16)]
        hm2 = [cv(AR2, 14336, 512, BF16), cv(AR2, 14848, 512, BF16)]
        xgT_b = cv(AR2, 15360, NPAD * 4, BF16)
        w2s2 = [cv(AR3, 0, 4096, BF16), cv(AR3, 4096, 4096, BF16)]
        u_b2 = [cv(AR3, 8192, 512, BF16), cv(AR3, 8704, 512, BF16)]
        xs_t2 = [cv(AR3, 9216, 512, BF16), cv(AR3, 9728, 512, BF16)]
        sz_t2 = [cv(AR3, 10240, 512, BF16), cv(AR3, 10752, 512, BF16)]
        y2b2 = [cv(AR3, 11264, 512, BF16), cv(AR3, 11776, 512, BF16)]
        h2 = [cv(AR3, 12288, 512, BF16), cv(AR3, 12800, 512, BF16)]
        dbx3 = [cv(AR3, 13312, 512, BF16), cv(AR3, 13824, 512, BF16), cv(AR3, 14336, 512, BF16)]
        prod2 = [cv(AR3, 14848, 512, BF16), cv(AR3, 15360, 512, BF16)]
        y_acc2 = cv(AR3, 15872, 512, BF16)
        ps_h = [PSA, PSB]
        ps_o = [PSC, PSD]

        NSEG = NTC * NDB

        def seg_of(si):
            return si // NDB, si % NDB

        def scan_sync_seg(si):
            tc, db = seg_of(si)
            if si == 0:
                for tcn in range(2):
                    p.op(lambda e, tcn=tcn: e.dma_start(
                        out=drq[tcn][:, :],
                        in_=drrow_h[0:1, tcn * TCH:(tcn + 1) * TCH]
                        .partition_broadcast(128).squeeze(1)),
                        inc=("sdr0" if tcn % 2 == 0 else "sdr1", ("dr", tcn), 16))
            par = si % 2
            if si >= 2:
                p.wait("v4", ("y2", si - 2))
            sem = "ssg0" if par == 0 else "ssg1"
            p.op(lambda e, tc=tc, db=db: e.dma_start(
                out=xs_t2[par][:, :], in_=xs_hbm[:, db * S + tc * TCH:db * S + (tc + 1) * TCH]),
                inc=(sem, ("xs", si), 16))
            p.op(lambda e, tc=tc, db=db: e.dma_start(
                out=sz_t2[par][:, :], in_=sz_hbm[:, db * S + tc * TCH:db * S + (tc + 1) * TCH]),
                inc=(sem, ("sg", si), 16))

            def bc_load(qn, which_bc):
                gq = si * 4 + qn
                h = cbc_h if which_bc == "C" else bbc_h
                sems = ("scq0", "scq1") if which_bc == "C" else ("sbq0", "sbq1")
                if gq >= 2:
                    psi, pqn = divmod(gq - 2, 4)
                    if which_bc == "C":
                        p.wait("v4", ("prodq", psi, pqn))
                    else:
                        p.wait("vself", ("bx", psi, pqn * 4 + 3))
                dst = (Cq if which_bc == "C" else Bq)[qn % 2]
                p.op(lambda e, h=h, dst=dst, tc=tc, qn=qn: e.dma_start(
                    out=dst[:, :],
                    in_=h.rearrange("(a r) t -> a (r t)", a=1)
                    [0:1, tc * 16 * TCH + qn * 4 * TCH:tc * 16 * TCH + (qn + 1) * 4 * TCH]
                    .partition_broadcast(128).squeeze(1)),
                    inc=(sems[qn % 2], (which_bc, si, qn), 16))

            for qn in range(4):
                bc_load(qn, "B")
                bc_load(qn, "C")
            if si >= 1:
                ptc, pdb = seg_of(si - 1)
                p.wait("v4", ("y2", si - 1))
                p.op(lambda e, ptc=ptc, pdb=pdb, si=si: e.dma_start(
                    out=y2_hbm[:, pdb * S + ptc * TCH:pdb * S + (ptc + 1) * TCH],
                    in_=y2b2[(si - 1) % 2][:, :]),
                    inc=("gy0" if (si - 1) % 2 == 0 else "gy1", ("y2st", si - 1), 16))
            if si % NDB == NDB - 1 and si // NDB + 2 < NTC:
                tcn = si // NDB + 2
                p.wait("v4", ("x0", (tcn - 2) * NDB + NDB - 1))
                p.op(lambda e, tcn=tcn: e.dma_start(
                    out=drq[tcn % 2][:, :],
                    in_=drrow_h[0:1, tcn * TCH:(tcn + 1) * TCH]
                    .partition_broadcast(128).squeeze(1)),
                    inc=("sdr0" if tcn % 2 == 0 else "sdr1", ("dr", tcn), 16))

        def prep_x0ab(sj):
            tcj, dbj = seg_of(sj)
            def f():
                p.wait("sdr0" if tcj % 2 == 0 else "sdr1", ("dr", tcj))
                if sj >= 2:
                    p.wait("a4", ("ln", sj - 2))
                p.op(lambda e, tcj=tcj, dbj=dbj, sj=sj: e.tensor_scalar(
                    x02[sj % 2][:, :], drq[tcj % 2][:, :], dtw_col[:, dbj:dbj + 1],
                    dtb_col[:, dbj:dbj + 1], op0=ALU.mult, op1=ALU.add),
                    inc=("v4", ("x0", sj), 1))
                p.wait("v4", ("x0", sj))
                p.op(lambda e, sj=sj: e.scalar_tensor_tensor(
                    ab2[sj % 2][:, :], x02[sj % 2][:, :], cM1, x02[sj % 2][:, :],
                    op0=ALU.mult, op1=ALU.max), inc=("v4", ("ab", sj), 1))
            return f

        def scan_v_seg_ops(si):
            tc, db = seg_of(si)
            par = si % 2
            sem = "ssg0" if par == 0 else "ssg1"
            ops = []

            def first():
                p.wait(sem, ("sg", si))
            ops.append(first)

            if si + 1 < NSEG:
                ops.append(prep_x0ab(si + 1))
            else:
                ops.append(lambda: None)

            def prep_delta():
                p.wait("a4", ("ln", si))
                if si >= 2:
                    p.wait("a4", ("dAx", si - 2, DS - 1))
                p.op(lambda e, si=si: e.scalar_tensor_tensor(
                    delta2[si % 2][:, :], x02[si % 2][:, :], zcol, ab2[si % 2][:, :],
                    op0=ALU.max, op1=ALU.add),
                    inc=("v4", ("d", si), 1))
            ops.append(prep_delta)

            def prep_u():
                p.wait("v4", ("d", si))
                p.op(lambda e, si=si, par=par: e.tensor_tensor(
                    u_b2[par][:, :], delta2[si % 2][:, :], xs_t2[par][:, :], op=ALU.mult),
                    inc=("v4", ("uw", si), 1))
            ops.append(prep_u)

            def em_dbx(k):
                def f():
                    if k == 0:
                        p.wait("v4", ("uw", si))
                    p.wait("sbq0" if (k // 4) % 2 == 0 else "sbq1", ("B", si, k // 4))
                    g = si * DS + k
                    if g >= 3:
                        ptc2, r2 = divmod(g - 3, NDB * DS)
                        pdb2, pn2 = divmod(r2, DS)
                        p.wait("v4", ("scan", ptc2, pdb2, pn2))
                    p.op(lambda e, k=k: e.tensor_tensor(
                        dbx3[k % 3][:, :], u_b2[par][:, :],
                        Bq[(k // 4) % 2][:, (k % 4) * TCH:(k % 4 + 1) * TCH],
                        op=ALU.mult), inc=("vself", ("bx", si, k), 1))
                ops.append(f)

            def em_scan(k):
                def f():
                    p.wait("a4", ("dAx", si, k))
                    p.wait("vself", ("bx", si, k))
                    if k >= 2:
                        p.wait("vself", ("cy", si, k - 2))
                    p.op(lambda e, k=k, db=db: e.tensor_tensor_scan(
                        h2[k % 2][:, :], dA3[k % 5][:, :], dbx3[k % 3][:, :],
                        carry[:, db * DS + k:db * DS + k + 1], op0=ALU.mult, op1=ALU.add),
                        inc=("v4", ("scan", tc, db, k), 1))
                ops.append(f)

            def prtag(k):
                if k % 4 == 3:
                    return ("v4", ("prodq", si, k // 4))
                return ("vself", ("pr", si, k))

            def em_prod(k):
                def f():
                    p.wait("scq0" if (k // 4) % 2 == 0 else "scq1", ("C", si, k // 4))
                    p.wait("v4", ("scan", tc, db, k))
                    if k >= 2:
                        p.wait("vself", ("ya", si, k - 2))
                    elif si >= 1 and k == 1:
                        p.wait("vself", ("ya", si - 1, DS - 1))
                    elif si >= 1 and k == 0:
                        p.wait("v4", ("y2", si - 1))
                    sm, tag = prtag(k)
                    p.op(lambda e, k=k: e.tensor_tensor(
                        prod2[k % 2][:, :], h2[k % 2][:, :],
                        Cq[(k // 4) % 2][:, (k % 4) * TCH:(k % 4 + 1) * TCH], op=ALU.mult),
                        inc=(sm, tag, 1))
                ops.append(f)

            def em_add(k):
                def f():
                    sm, tag = prtag(k)
                    p.wait(sm, tag)
                    if k >= 1:
                        p.wait("vself", ("ya", si, k - 1))
                    if k == 0:
                        if si >= 2:
                            p.wait("vself", ("yt", si - 2))
                        p.op(lambda e: e.tensor_copy(y_acc2[:, :], prod2[0][:, :]),
                             inc=("vself", ("ya", si, 0), 1))
                    else:
                        p.op(lambda e, k=k: e.tensor_tensor(y_acc2[:, :], y_acc2[:, :],
                                                            prod2[k % 2][:, :], op=ALU.add),
                             inc=("vself", ("ya", si, k), 1))
                ops.append(f)

            def em_carry(k):
                def f():
                    p.op(lambda e, k=k, db=db: e.tensor_copy(
                        carry[:, db * DS + k:db * DS + k + 1], h2[k % 2][:, 1023:1024]),
                        inc=("vself", ("cy", si, k), 1))
                ops.append(f)

            em_dbx(0)
            em_dbx(1)
            for k in range(DS):
                if k + 2 < DS:
                    em_dbx(k + 2)
                em_scan(k)
                if k >= 1:
                    em_prod(k - 1)
                    em_carry(k - 1)
                if k >= 2:
                    em_add(k - 2)
            em_prod(DS - 1)
            em_carry(DS - 1)
            em_add(DS - 2)
            em_add(DS - 1)

            def fin1():
                p.wait("vself", ("ya", si, DS - 1))
                p.op(lambda e, db=db: e.scalar_tensor_tensor(
                    prod2[0][:, :], xs_t2[par][:, :], d_col[:, db:db + 1], y_acc2[:, :],
                    op0=ALU.mult, op1=ALU.add), inc=("vself", ("yt", si), 1))
            ops.append(fin1)

            def fin2():
                p.wait("vself", ("yt", si))
                if si >= 2:
                    p.wait("gy0" if par == 0 else "gy1", ("y2st", si - 2))
                p.op(lambda e: e.tensor_tensor(y2b2[par][:, :], prod2[0][:, :], sz_t2[par][:, :],
                                               op=ALU.mult),
                     inc=("v4", ("y2", si), 1))
            ops.append(fin2)
            return ops

        def ffn_emit_w(e, fb):
            fbg = e * 4 + fb
            fpar = fbg % 2
            gsem = "g60" if fpar == 0 else "g61"
            if fbg >= 2:
                pe_, pf = divmod(fbg - 2, 4)
                p.wait("t6", ("o2", pe_ * HPE + pf * NG + NG - 1))
            if fb == 0:
                p.op(lambda e2, e=e: e2.dma_start(out=b2cur[e % 2][:, :],
                                                  in_=b2_h[0:1, e * D:(e + 1) * D]),
                     inc=(gsem, ("b2", e), 16))
            for kt in range(8):
                p.op(lambda e2, e=e, fb=fb, kt=kt, fpar=fpar: e2.dma_start(
                    out=w1s2[fpar][:, kt * 1024:(kt + 1) * 1024],
                    in_=w1_h[e, kt * 128:(kt + 1) * 128, fb * 1024:(fb + 1) * 1024]),
                    inc=(gsem, ("w1", fbg, kt), 16))
            for ft in range(8):
                inc = (gsem, ("w", fbg), 16) if ft == 7 else (gsem, ("w2", fbg, ft), 16)
                p.op(lambda e2, e=e, fb=fb, ft=ft, fpar=fpar: e2.dma_start(
                    out=w2s2[fpar][:, ft * 1024:(ft + 1) * 1024],
                    in_=w2_h[e, fb * 1024 + ft * 128:fb * 1024 + (ft + 1) * 128, :]),
                    inc=inc)

        def ffn_gp():
            for r in range(8):
                p.op(lambda e2, r=r: e2.dma_start(
                    out=idx16[r * 16:(r + 1) * 16, :],
                    in_=perm_h.rearrange("(q c) one -> q (c one)", q=16)),
                    inc=("gidx", ("idx", r), 16))
            p.wait("gidx", ("idx", 7))
            def emit_pair_scatter(m):
                p.wait("v6", ("ev", (2 * m + 2) * HPE - 1))
                p.op(lambda e2, m=m: e2.dma_scatter_add(
                    out_ap=acc_hbm[:, :],
                    in_ap=oacc16.rearrange("q (b d) -> q b d", b=2 * NG),
                    idxs_ap=idx16[:, 2 * m * (NPAD // 16):(2 * m + 2) * (NPAD // 16)],
                    num_idxs=2 * NPAD, num_idxs_reg=2 * NPAD, elem_size=D),
                    inc=("gsc", ("sc", m), 16))

            for e in range(NE):
                if e >= 1:
                    p.wait("t6", ("hb", e * HPE - 1))
                p.op(lambda e2, e=e: e2.dma_gather(
                    out_ap=xgT_b.rearrange("q (c t) -> q c t", c=8),
                    in_ap=xn_tm_h[:, :],
                    idxs_ap=idx16[:, e * (NPAD // 16):(e + 1) * (NPAD // 16)],
                    num_idxs=NPAD, num_idxs_reg=NPAD, elem_size=D,
                    transpose=True), inc=("ggx0", ("xg", e), 16))
                ffn_emit_w(e, 0)
                ffn_emit_w(e, 1)
                if e >= 2 and e % 2 == 0:
                    emit_pair_scatter(e // 2 - 1)
                ffn_emit_w(e, 2)
                ffn_emit_w(e, 3)
            emit_pair_scatter(NE // 2 - 1)
            p.wait("gsc", ("sc", NE // 2 - 1))
            p.wait("ggx0", ("xg", NE - 1))
            p.wait("g60", ("w", 4 * NE - 2))
            p.wait("g61", ("w", 4 * NE - 1))

        def ffn_pe():
            for e in range(NE):
                for fb in range(4):
                    fbg = e * 4 + fb
                    p.wait("g60" if fbg % 2 == 0 else "g61", ("w", fbg))
                    if fb == 0:
                        p.wait("ggx0", ("xg", e))
                    for g in range(NG):
                        hseq = e * HPE + fb * NG + g
                        hpar = hseq % 2
                        if hseq >= 2:
                            p.wait("a6", ("gel", hseq - 2))
                        nmm = 0
                        for ft in range(8):
                            for kt in range(8):
                                nmm += 1
                                inc = ("t6", ("hb", hseq), 1) if nmm == 64 else None
                                p.op(lambda e2, hpar=hpar, fbg=fbg, ft=ft, kt=kt, g=g: e2.matmul(
                                    ps_h[hpar][:, ft * 128:(ft + 1) * 128],
                                    w1s2[fbg % 2][:, kt * 1024 + ft * 128:kt * 1024 + ft * 128 + 128],
                                    xgT_b[:, kt * NPAD + g * 128:kt * NPAD + (g + 1) * 128],
                                    start=(kt == 0), stop=(kt == 7)), inc=inc)
                        p.wait("a6", ("gel", hseq))
                        if hseq >= 2:
                            p.wait("v6", ("ev", hseq - 2))
                        nmm = 0
                        tot = 16 + (2 if fb == 0 else 0)
                        if fb == 0:
                            for dmh in range(2):
                                nmm += 1
                                p.op(lambda e2, hpar=hpar, dmh=dmh, e=e: e2.matmul(
                                    ps_o[hpar][:, dmh * 512:(dmh + 1) * 512], ones1[:, :],
                                    b2cur[e % 2][:, dmh * 512:(dmh + 1) * 512],
                                    start=True, stop=False))
                        for ft in range(8):
                            for dmh in range(2):
                                nmm += 1
                                inc = ("t6", ("o2", hseq), 1) if nmm == tot else None
                                p.op(lambda e2, hpar=hpar, fbg=fbg, ft=ft, dmh=dmh, fb=fb: e2.matmul(
                                    ps_o[hpar][:, dmh * 512:(dmh + 1) * 512],
                                    hm2[hpar][:, ft * 128:(ft + 1) * 128],
                                    w2s2[fbg % 2][:, ft * 1024 + dmh * 512:ft * 1024 + dmh * 512 + 512],
                                    start=(fb != 0 and ft == 0), stop=(ft == 7)), inc=inc)

        def ffn_act_hseq(hseq):
            e, r = divmod(hseq, HPE)
            fb, g = divmod(r, NG)
            hpar = hseq % 2
            p.wait("t6", ("hb", hseq))
            if hseq >= 2:
                p.wait("t6", ("o2", hseq - 2))
            for ft in range(8):
                inc = ("a6", ("gel", hseq), 1) if ft == 7 else None
                p.op(lambda e2, hpar=hpar, ft=ft, e=e, fb=fb: e2.activation(
                    hm2[hpar][:, ft * 128:(ft + 1) * 128],
                    ps_h[hpar][:, ft * 128:(ft + 1) * 128], AF.Gelu,
                    bias=b1_col[:, e * 32 + fb * 8 + ft:e * 32 + fb * 8 + ft + 1]), inc=inc)

        def act_expln(sj):
            p.wait("v4", ("ab", sj))
            p.op(lambda e, sj=sj: e.activation(e_sb[:, :], ab2[sj % 2][:, :], AF.Exp, scale=cM1),
                 inc=("aself", ("e4", sj), 1))
            p.wait("aself", ("e4", sj))
            p.op(lambda e, sj=sj: e.activation(ab2[sj % 2][:, :], e_sb[:, :], AF.Ln, bias=cONE),
                 inc=("a4", ("ln", sj), 1))

        def scan_act(si):
            p.wait("v4", ("d", si))
            for n in range(DS):
                gn = si * DS + n
                if gn >= 5:
                    ptc2, r2 = divmod(gn - 5, NDB * DS)
                    pdb2, pn2 = divmod(r2, DS)
                    p.wait("v4", ("scan", ptc2, pdb2, pn2))
                p.op(lambda e, si=si, n=n: e.activation(
                    dA3[n % 5][:, :], delta2[si % 2][:, :], AF.Exp, scale=a_col[:, n:n + 1]),
                    inc=("a4", ("dAx", si, n), 1))
            if si + 1 < NSEG:
                act_expln(si + 1)

        def ffn_v_thunks():
            ops = []
            for e in range(NE):
                for fb in range(4):
                    for g in range(NG):
                        hseq = e * HPE + fb * NG + g
                        gh = (e % 2) * NG + g
                        def ev(hseq=hseq, e=e, fb=fb, gh=gh):
                            p.wait("t6", ("o2", hseq))
                            if fb == 0:
                                if e >= 2:
                                    p.wait("gsc", ("sc", e // 2 - 1))
                                p.op(lambda e2, gh=gh, hseq=hseq: e2.tensor_copy(
                                    oacc16[:, gh * 1024:(gh + 1) * 1024], ps_o[hseq % 2][:, :]),
                                    inc=("v6", ("ev", hseq), 1))
                            else:
                                p.wait("v6", ("ev", hseq - NG))
                                p.op(lambda e2, gh=gh, hseq=hseq: e2.tensor_tensor(
                                    oacc16[:, gh * 1024:(gh + 1) * 1024], ps_o[hseq % 2][:, :],
                                    oacc16[:, gh * 1024:(gh + 1) * 1024], op=ALU.add),
                                    inc=("v6", ("ev", hseq), 1))
                        ops.append(ev)
            return ops

        def p46(which):
            if which == "sync":
                for si in range(NSEG):
                    scan_sync_seg(si)
                ptc, pdb = seg_of(NSEG - 1)
                p.wait("v4", ("y2", NSEG - 1))
                p.op(lambda e, ptc=ptc, pdb=pdb: e.dma_start(
                    out=y2_hbm[:, pdb * S + ptc * TCH:pdb * S + (ptc + 1) * TCH],
                    in_=y2b2[(NSEG - 1) % 2][:, :]),
                    inc=("gy1", ("y2st", NSEG - 1), 16))
                p.wait("gy0", ("y2st", NSEG - 2))
                p.wait("gy1", ("y2st", NSEG - 1))
                p.wait("ssg0", ("sg", NSEG - 2))
                p.wait("ssg1", ("sg", NSEG - 1))
                p.wait("scq0", ("C", NSEG - 1, 2))
                p.wait("scq1", ("C", NSEG - 1, 3))
                p.wait("sbq0", ("B", NSEG - 1, 2))
                p.wait("sbq1", ("B", NSEG - 1, 3))
                p.wait("sdr0", ("dr", 2))
                p.wait("sdr1", ("dr", 3))
            elif which == "gpsimd":
                ffn_gp()
            elif which == "vector":
                EV = ffn_v_thunks()
                prep_x0ab(0)()
                for si in range(NSEG):
                    seg = scan_v_seg_ops(si)
                    for opf in seg[:23]:
                        opf()
                    if si >= 1:
                        EV[(si - 1) * 3]()
                        EV[(si - 1) * 3 + 1]()
                    for opf in seg[23:28]:
                        opf()
                    if si >= 1:
                        EV[(si - 1) * 3 + 2]()
                    for opf in seg[28:]:
                        opf()
                for j in range(3):
                    EV[(NSEG - 1) * 3 + j]()
            elif which == "scalar":
                act_expln(0)
                for si in range(NSEG):
                    scan_act(si)
                    for j in range(3):
                        ffn_act_hseq(si * 3 + j)
            elif which == "tensor":
                ffn_pe()

        owT = cv(AR1, 0, 8192, BF16)
        y2sb = [cv(AR1, 8192, 4096, BF16), cv(AR3, 0, 4096, BF16), cv(AR3, 4096, 4096, BF16)]
        owbf = [cv(AR2, 0, 1024, BF16), cv(AR2, 1024, 1024, BF16)]
        acc_t = [cv(AR2, 2048, 512, BF16), cv(AR2, 2560, 512, BF16)]
        outb = [cv(AR2, 4096, 1024), cv(AR2, 5120, 1024)]
        ps_out = [PSA, PSB]
        ps_ow = PSC[:, :].bitcast(BF16)

        def p5(which):
            if which == "gpsimd":
                for mr in range(8):
                    if mr >= 2:
                        p.wait("t5", ("owT", mr - 2))
                    p.op(lambda e, mr=mr: e.dma_start(out=owbf[mr % 2][:, :],
                                                      in_=outw_h[mr * 128:(mr + 1) * 128, :]),
                         inc=("gw0" if mr % 2 == 0 else "gw1", ("ow", mr), 16))
                for g in range(32):
                    p.wait("v5", ("opev", g))
                    p.op(lambda e, g=g: e.dma_start(out=out_h[g * 128:(g + 1) * 128, :],
                                                    in_=outb[g % 2][:, :]),
                         inc=("sout0" if g % 2 == 0 else "sout1", ("o5", g), 16))
                p.wait("gw0", ("ow", 6))
                p.wait("gw1", ("ow", 7))
                p.wait("sout0", ("o5", 30))
                p.wait("sout1", ("o5", 31))
            elif which == "tensor":
                for mr in range(8):
                    p.wait("gw0" if mr % 2 == 0 else "gw1", ("ow", mr))
                    if mr >= 1:
                        p.wait("v5", ("owev", mr - 1))
                    for kt in range(16):
                        inc = ("t5", ("owT", mr), 1) if kt == 15 else None
                        p.op(lambda e, mr=mr, kt=kt: e.transpose(
                            ps_ow[:, kt * 128:(kt + 1) * 128], owbf[mr % 2][:, kt * 128:(kt + 1) * 128],
                            identb[:, :]), inc=inc)
                p.wait("v5", ("owev", 7))
                for c in range(8):
                    for tt in range(4):
                        g = c * 4 + tt
                        p.wait("sy2%d" % (c % 3), ("y2", c, 15))
                        if g >= 2:
                            p.wait("v5", ("opev", g - 2))
                        nmm = 0
                        for kt in range(16):
                            for hh in range(2):
                                nmm += 1
                                inc = ("t5", ("op", g), 1) if nmm == 32 else None
                                p.op(lambda e, c=c, tt=tt, g=g, kt=kt, hh=hh: e.matmul(
                                    ps_out[g % 2][:, hh * 512:(hh + 1) * 512],
                                    y2sb[c % 3][:, kt * 512 + tt * 128:kt * 512 + (tt + 1) * 128],
                                    owT[:, kt * 1024 + hh * 512:kt * 1024 + (hh + 1) * 512],
                                    start=(kt == 0), stop=(kt == 15)), inc=inc)
            elif which == "sync":
                for c in range(8):
                    if c >= 3:
                        p.wait("t5", ("op", (c - 3) * 4 + 3))
                    for kt in range(16):
                        p.op(lambda e, c=c, kt=kt: e.dma_start(
                            out=y2sb[c % 3][:, kt * 512:(kt + 1) * 512],
                            in_=y2_hbm[:, kt * S + c * 512:kt * S + (c + 1) * 512]),
                            inc=("sy2%d" % (c % 3), ("y2", c, kt), 16))
                    for tt in range(4):
                        g = c * 4 + tt
                        if g >= 2:
                            p.wait("v5", ("opev", g - 2))
                        p.op(lambda e, g=g: e.dma_start(out=acc_t[g % 2][:, :],
                                                        in_=acc_hbm[g * 128:(g + 1) * 128, :]),
                             inc=("sal0" if g % 2 == 0 else "sal1", ("acc", g), 16))
                p.wait("sal0", ("acc", 30))
                p.wait("sal1", ("acc", 31))
            elif which == "vector":
                for mr in range(8):
                    p.wait("t5", ("owT", mr))
                    p.op(lambda e, mr=mr: e.tensor_copy(
                        owT.rearrange("z (kt m) -> z kt m", kt=16)[:, :, mr * 128:(mr + 1) * 128],
                        ps_ow.rearrange("z (kt m) -> z kt m", kt=16)),
                        inc=("v5", ("owev", mr), 1))
                for c in range(8):
                    for tt in range(4):
                        g = c * 4 + tt
                        p.wait("t5", ("op", g))
                        p.wait("sal0" if g % 2 == 0 else "sal1", ("acc", g))
                        if g >= 2:
                            p.wait("sout0" if g % 2 == 0 else "sout1", ("o5", g - 2))
                        p.op(lambda e, g=g: e.tensor_tensor(outb[g % 2][:, :], ps_out[g % 2][:, :],
                                                            acc_t[g % 2][:, :], op=ALU.add),
                             inc=("v5", ("opev", g), 1))


        DBG.update(dict(oh_tok=oh_tok[:, :], xn2_sb=xn2_sb[:, :], carry=carry[:, :],
                        a_col=a_col[:, :], b1_col=b1_col[:, :], oh_full=oh_tok,
                        idx16=idx16, Bq0=Bq[0], Bq1=Bq[1], Cq0=Cq[0], Cq1=Cq[1],
                        delta20=delta2[0], delta21=delta2[1], drq0=drq[0], drq1=drq[1],
                        x0t=x02[0], abt=ab2[0], e_sb=e_sb, dA30=dA3[0], oacc16=oacc16,
                        xgT_b=xgT_b, bbc_h=bbc_h, cbc_h=cbc_h))

        def engine_prog(which):
            plist = [p0, p1, p2, p3, p46, p5]
            for k in range(min(phases, 6)):
                plist[k](which)
                if k < 5:
                    p.barrier(k + 1)

        p.start_count()
        for w in ["sync", "gpsimd", "vector", "scalar", "tensor"]:
            engine_prog(w)

        semhandles = {s: es.enter_context(nc.semaphore(s)) for s in SEMS}
        p.start_emit(semhandles)
        blk = es.enter_context(nc.Block())

        @blk.sync
        def _(e):
            p.eng = e
            engine_prog("sync")

        @blk.gpsimd
        def _(e):
            p.eng = e
            engine_prog("gpsimd")

        @blk.vector
        def _(e):
            p.eng = e
            engine_prog("vector")

        @blk.scalar
        def _(e):
            p.eng = e
            engine_prog("scalar")

        @blk.tensor
        def _(e):
            p.eng = e
            engine_prog("tensor")

    bacc.Bacc.insert_library_loads(nc)
    mybir.codegen_inst_isa_subclasses(nc)
    return nc


def kernel(**inputs):
    gi = lambda k: np.ascontiguousarray(np.asarray(inputs[k], dtype=np.float32))
    x = gi("L0_attn_x")
    base = {
        "ln_g": gi("L0_attn_ln_g").reshape(8, 128),
        "ln_b": gi("L0_attn_ln_b").reshape(8, 128),
        "in_w": gi("L0_attn_in_w"),
        "conv_w": gi("L0_attn_conv_w").reshape(DI, 4),
        "conv_b": gi("L0_attn_conv_b").reshape(16, 128),
        "xproj_w": gi("L0_attn_xproj_w"),
        "dt_w": gi("L0_attn_dt_w").reshape(16, 128),
        "dt_b": gi("L0_attn_dt_b").reshape(16, 128),
        "A_log": gi("L0_attn_A_log").reshape(1, DS),
        "Dvec": gi("L0_attn_D").reshape(16, 128),
        "out_w": gi("L0_attn_out_w"),
        "wr": gi("L0_ffn_wr"),
        "br": gi("L0_ffn_br").reshape(NE, 1),
        "w1": gi("L0_ffn_w1"),
        "b1": gi("L0_ffn_b1").reshape(NE * 32, 128),
        "w2": gi("L0_ffn_w2"),
        "b2": gi("L0_ffn_b2").reshape(1, NE * D),
    }
    nc = build_nc()
    in_maps = [dict(base, x=np.ascontiguousarray(x[c])) for c in range(8)]
    res = run_bass_kernel_spmd(nc, in_maps, core_ids=list(range(8)))
    return np.stack([np.asarray(res.results[c]["out"]) for c in range(8)], axis=0)


if __name__ == "__main__":
    nc = build_nc()
    print("build ok")
